# version 16
# baseline (speedup 1.0000x reference)
import numpy as np
from contextlib import ExitStack

import concourse.bass as bass
import concourse.mybir as mybir
from concourse.bass import IndirectOffsetOnAxis
from concourse.bass_utils import run_bass_kernel_spmd

F32 = mybir.dt.float32
BF16 = mybir.dt.bfloat16
I32 = mybir.dt.int32
AF = mybir.ActivationFunctionType
ALU = mybir.AluOpType
AX = mybir.AxisListType

THRESH = [0.2441942809699397, 0.46396913384288535, 0.6861859295255305,
          0.9059607823984762, 1.083978413225562]

BB, L, DM, FF = 4, 4096, 1024, 4096
LLOC = L // 2
DV = 512
DC = DM // 128
FC = FF // 128
NT = LLOC // 128
NCHUNK = L // 128
LN_EPS = 1e-5

CC1W = 1024
HBLK = 3328

ENGINES = ("tensor", "vector", "scalar", "sync", "gpsimd")


class Sem:
    def __init__(self, es, nc, name, eng=None):
        self.h = es.enter_context(nc.semaphore(name))
        self.n = 0
        self.eng = eng

    def at(self):
        return (self, self.n)


class Sched:

    def __init__(self, eng_sems=None):
        self.q = {e: [] for e in ENGINES}
        self.eng_sems = eng_sems or {}

    def add(self, eng, fn, waits=(), incs=(), dma=False):
        es = self.eng_sems.get(eng)
        waits = list(waits)
        incs = list(incs)
        if es is not None:
            waits.append(es.at())
            if not dma and not any(s is es for (s, k) in incs):
                incs.append((es, 1))
        w = [(s.h, v) for (s, v) in waits]
        il = []
        for (s, k) in incs:
            s.n += k
            il.append((s.h, k))
        self.q[eng].append((w, fn, il))

    def emit(self, block):
        for eng in ENGINES:
            items = self.q[eng]
            if not items:
                continue

            def body(e, items=items):
                for (waits, fn, incs) in items:
                    for (h, v) in waits:
                        if v > 0:
                            e.wait_ge(h, v)
                    instr = fn(e)
                    for (h, k) in incs:
                        assert instr is not None
                        instr = instr.then_inc(h, k)
            getattr(block, eng)(body)
        self.q = {e: [] for e in ENGINES}


def build_nc(upto="D"):
    nc = bass.Bass("TRN2", target_bir_lowering=False)

    x_in = nc.dram_tensor("x", [LLOC, DM], F32, kind="ExternalInput").ap()
    wfcT = nc.dram_tensor("wfcT", [DM, FF], F32, kind="ExternalInput").ap()
    wpjT = nc.dram_tensor("wpjT", [FF, DM], F32, kind="ExternalInput").ap()
    wqkv = nc.dram_tensor("wqkv", [DM, 3 * DM], F32, kind="ExternalInput").ap()
    wb_in = nc.dram_tensor("wb", [128, DC], F32, kind="ExternalInput").ap()
    gfull = nc.dram_tensor("gfull", [128, DM], F32, kind="ExternalInput").ap()
    bfull = nc.dram_tensor("bfull", [128, DM], F32, kind="ExternalInput").ap()
    ident_in = nc.dram_tensor("ident", [128, 128], F32, kind="ExternalInput").ap()
    twoid_in = nc.dram_tensor("twoid", [128, 128], F32, kind="ExternalInput").ap()
    masksu_in = nc.dram_tensor("masksu", [128, 128], F32, kind="ExternalInput").ap()
    maskiu_in = nc.dram_tensor("maskiu", [128, 128], F32, kind="ExternalInput").ap()
    idxfm_in = nc.dram_tensor("idxfm", [128, 64], I32, kind="ExternalInput").ap()
    idxktv_in = nc.dram_tensor("idxktv", [128, 32], I32, kind="ExternalInput").ap()
    idxbh_in = nc.dram_tensor("idxbh", [128, 4], I32, kind="ExternalInput").ap()
    idxo_in = nc.dram_tensor("idxo", [128, 32], I32, kind="ExternalInput").ap()
    idxv_in = nc.dram_tensor("idxv", [128, 32], I32, kind="ExternalInput").ap()
    idxxl_in = nc.dram_tensor("idxxl", [128, 32], I32, kind="ExternalInput").ap()
    y_out = nc.dram_tensor("y", [LLOC, DM], F32, kind="ExternalOutput").ap()

    x1_dram = nc.dram_tensor("x1_dram", [LLOC, DM], F32).ap()
    cc1_in = nc.dram_tensor("cc1_in", [2 * HBLK, CC1W], BF16)
    cc1_out = nc.dram_tensor("cc1_out", [2 * 8 * HBLK, CC1W], BF16, addr_space="Shared")
    cc1c_in = nc.dram_tensor("cc1c_in", [256, 16], F32)
    cc1c_out = nc.dram_tensor("cc1c_out", [2 * 8 * 128, 16], F32, addr_space="Shared")
    ccv_in = nc.dram_tensor("ccv_in", [2 * LLOC, DV], BF16)
    ccv_out = nc.dram_tensor("ccv_out", [2 * 8 * LLOC, DV], BF16, addr_space="Shared")
    cc2_in = nc.dram_tensor("cc2_in", [L, DV], BF16)
    cc2_out = nc.dram_tensor("cc2_out", [2 * 8 * LLOC, DV], BF16, addr_space="Shared")

    cc1v = cc1_in.ap()
    qfw = [cc1v[g * HBLK + 1024:g * HBLK + 2048, :].rearrange(
        "(c p) t -> p c t", p=128) for g in range(2)]
    kfw = [cc1v[g * HBLK + 2048:g * HBLK + 3072, :].rearrange(
        "(c p) t -> p c t", p=128) for g in range(2)]

    es = ExitStack()
    S_PE = Sem(es, nc, "s_pe")
    S_DVE = Sem(es, nc, "s_dve")
    S_ACT = Sem(es, nc, "s_act")
    S_GP = Sem(es, nc, "s_gp")

    DSEMS = []

    def dsem(name, eng):
        s = Sem(es, nc, "q_" + name, eng)
        DSEMS.append(s)
        return s

    CONS = dsem("cons", "sync")
    CONG = dsem("cong", "gpsimd")
    WFC = [dsem("wfc0", "gpsimd"), dsem("wfc1", "gpsimd")]
    WPJ = [dsem("wpj0", "gpsimd"), dsem("wpj1", "gpsimd")]
    WQKV = dsem("wqkv", "gpsimd")
    XLD = [dsem(f"xld{i}", "sync") for i in range(3)]
    X1O = dsem("x1o", "sync")
    DLD = [dsem("dld0", "gpsimd"), dsem("dld1", "gpsimd")]
    DXL = [dsem("dxl0", "sync"), dsem("dxl1", "sync")]
    QST = dsem("qst", "scalar")
    KFST = dsem("kfst", "scalar")
    KTST = [dsem("ktst0", "scalar"), dsem("ktst1", "scalar")]
    VTST = [dsem("vtst0", "scalar"), dsem("vtst1", "scalar")]
    CHT = [dsem("cht0", "gpsimd"), dsem("cht1", "gpsimd")]
    OST = [dsem("ost0", "scalar"), dsem("ost1", "scalar")]
    BRDY = dsem("brdy", "scalar")
    XLST = [dsem("xlst0", "scalar"), dsem("xlst1", "scalar")]
    CC = dsem("cc", "gpsimd")
    CCA = dsem("cca", "gpsimd")
    CCB = dsem("ccb", "gpsimd")
    SQG = [dsem("sqg0", "gpsimd"), dsem("sqg1", "gpsimd")]
    BH2S = [dsem(f"bh2s{i}", "gpsimd") for i in range(4)]

    id_b = es.enter_context(nc.sbuf_tensor("id_b", [128, 128], BF16))
    wb_sb = es.enter_context(nc.sbuf_tensor("wb_sb", [128, DC], BF16))
    bh2 = es.enter_context(nc.sbuf_tensor("bh2", [128, 64], F32))
    idxfm = es.enter_context(nc.sbuf_tensor("idxfm_s", [128, 64], I32))
    idxktv = es.enter_context(nc.sbuf_tensor("idxktv_s", [128, 32], I32))
    idxbh = es.enter_context(nc.sbuf_tensor("idxbh_s", [128, 4], I32))
    idxo = es.enter_context(nc.sbuf_tensor("idxo_s", [128, 32], I32))
    idxv = es.enter_context(nc.sbuf_tensor("idxv_s", [128, 32], I32))
    idxxl = es.enter_context(nc.sbuf_tensor("idxxl_s", [128, 32], I32))

    fb_guard = [0]


    def load_consts(sc):
        for (dst, src) in ((id_b, ident_in), (wb_sb, wb_in), (idxfm, idxfm_in),
                           (idxktv, idxktv_in), (idxbh, idxbh_in),
                           (idxo, idxo_in), (idxv, idxv_in), (idxxl, idxxl_in)):
            sc.add("gpsimd", lambda e, dst=dst, src=src: e.dma_start(
                out=dst[:], in_=src), incs=[(CONG, 16)], dma=True)

    def load_ffn_weights(sc, wfc_sb, wpj_sb):
        wfc_v = wfcT.rearrange("(c p) f -> p c f", p=128)
        wpj_v = wpjT.rearrange("(c p) f -> p c f", p=128)
        wfc3 = wfc_sb[:].rearrange("p (c f) -> p c f", c=DC)
        wpj3 = wpj_sb[:].rearrange("p (c f) -> p c f", c=FC)
        for half in range(2):
            fs = slice(half * (FF // 2), (half + 1) * (FF // 2))
            sc.add("gpsimd", lambda e, fs=fs: e.dma_start(
                out=wfc3[:, :, fs], in_=wfc_v[:, :, fs]),
                incs=[(WFC[half], 16)], dma=True)
        for half in range(2):
            cs = slice(half * (FC // 2), (half + 1) * (FC // 2))
            sc.add("gpsimd", lambda e, cs=cs: e.dma_start(
                out=wpj3[:, cs, :], in_=wpj_v[:, cs, :]),
                incs=[(WPJ[half], 16)], dma=True)

    def emit_ln(sc, x_ap, xn_ap, stats, slot, waits):
        st6 = stats["st6"]; st2 = stats["st2"]
        slot = slot % 4
        s6 = slice(slot * 12, slot * 12 + 12)
        mu = st2[:, 2 * slot:2 * slot + 1]
        var = st2[:, 2 * slot + 1:2 * slot + 2]
        rstd = stats["rstd"][:, slot:slot + 1]
        sc.add("vector", lambda e: e.bn_stats(
            st6[:, slot * 12:slot * 12 + 6], x_ap[:, 0:512]),
            waits=waits, incs=[(S_DVE, 1)])
        sc.add("vector", lambda e: e.bn_stats(
            st6[:, slot * 12 + 6:slot * 12 + 12], x_ap[:, 512:1024]),
            waits=waits, incs=[(S_DVE, 1)])
        sc.add("vector", lambda e: e.bn_aggr(st2[:, 2 * slot:2 * slot + 2],
                                             st6[:, s6]),
               waits=[S_DVE.at()], incs=[(S_DVE, 1)])
        sc.add("vector", lambda e: e.tensor_scalar_add(var, var, LN_EPS),
               waits=[S_DVE.at()], incs=[(S_DVE, 1)])
        sc.add("scalar", lambda e: e.activation(stats["lnt"][:, slot:slot + 1],
                                                var, AF.Ln),
               waits=[S_DVE.at()], incs=[(S_ACT, 1)])
        sc.add("scalar", lambda e: e.activation(rstd, stats["lnt"][:, slot:slot + 1],
                                                AF.Exp, scale=-0.5),
               waits=[S_ACT.at()], incs=[(S_ACT, 1)])
        sc.add("vector", lambda e: e.tensor_scalar(xn_ap, x_ap, mu, rstd,
                                                   ALU.subtract, ALU.mult),
               waits=[S_ACT.at()], incs=[(S_DVE, 1)])

    def emit_transpose_tile(sc, src_fn, dst_fm_ap, pst_ap, waits):
        def pe_ops(e):
            ins = None
            for c in range(DC):
                ins = e.transpose(pst_ap[:, c * 128:(c + 1) * 128], src_fn(c), id_b[:])
            return ins
        sc.add("tensor", pe_ops, waits=list(waits) + [(CONG, 128)],
               incs=[(S_PE, 1)])
        sc.add("vector", lambda e: e.tensor_copy(dst_fm_ap, pst_ap[:]),
               waits=[S_PE.at()], incs=[(S_DVE, 1)])

    def emit_stair_evac(sc, ps_y_list, bufs, waits):
        ybf = bufs[0]
        for ts, ps in enumerate(ps_y_list):
            sc.add("scalar", lambda e, ts=ts, ps=ps: e.activation(
                ybf[:, ts * DM:(ts + 1) * DM], ps, AF.Copy),
                waits=list(waits) + [(S_DVE, fb_guard[0])], incs=[(S_ACT, 1)])
        return S_ACT.at()

    def stair_dve_steps(sc, x_res_ap, out_ap, bufs, evac_at, out_waits):
        ybf, accA, accB = bufs

        def s0():
            sc.add("vector", lambda e: e.tensor_single_scalar(
                accA[:], ybf[:], THRESH[0], ALU.is_gt), waits=[evac_at])

        def mk(dst, src, k):
            def s():
                sc.add("vector", lambda e: e.scalar_tensor_tensor(
                    dst[:], ybf[:], THRESH[k], src[:], ALU.is_gt, ALU.add))
            return s

        def sf():
            sc.add("vector", lambda e: e.scalar_tensor_tensor(
                out_ap, accA[:], 0.1, x_res_ap, ALU.mult, ALU.add),
                waits=out_waits, incs=[(S_DVE, 1)])
            fb_guard[0] = S_DVE.n
        return [s0, mk(accB, accA, 1), mk(accA, accB, 2), mk(accB, accA, 3),
                mk(accA, accB, 4), sf]

    def phase_end(sc, exclude=()):
        for s in DSEMS:
            if s in exclude:
                continue
            sc.add(s.eng, lambda e: e.nop(), waits=[s.at()])

    def alloc_ffn_bufs(ph, wfc_sb, wpj_sb, px):
        fb = {"wfc": wfc_sb, "wpj": wpj_sb}
        fb["xst"] = [ph.enter_context(nc.sbuf_tensor(f"{px}xst{i}", [128, 2 * DM], F32))
                     for i in range(3)]
        x1o = ph.enter_context(nc.sbuf_tensor(f"{px}x1o", [128, 2 * DM], F32))
        fb["x1o"] = [x1o, x1o]
        fb["xnbf"] = [ph.enter_context(nc.sbuf_tensor(f"{px}xnbf{i}", [128, DM], BF16))
                      for i in range(2)]
        fb["xnfm"] = [ph.enter_context(nc.sbuf_tensor(f"{px}xnfm{i}", [128, DC * 256], BF16))
                      for i in range(2)]
        fb["hrelu"] = [ph.enter_context(nc.sbuf_tensor(f"{px}hrelu{i}", [128, 256], BF16))
                       for i in range(2)]
        fb["hsq"] = [ph.enter_context(nc.sbuf_tensor(f"{px}hsq{i}", [128, 256], BF16))
                     for i in range(2)]
        fb["stats"] = {
            "st6": ph.enter_context(nc.sbuf_tensor(px + "sst6", [128, 48], F32)),
            "st2": ph.enter_context(nc.sbuf_tensor(px + "sst2", [128, 16], F32)),
            "rstd": ph.enter_context(nc.sbuf_tensor(px + "srstd", [128, 8], F32)),
            "lnt": ph.enter_context(nc.sbuf_tensor(px + "slnt", [128, 8], F32)),
        }
        fb["stair"] = tuple(ph.enter_context(
            nc.sbuf_tensor(f"{px}stb{k}", [128, 2 * DM], BF16)) for k in range(3))
        fb["pst"] = [ph.enter_context(nc.psum_tensor(f"{px}pst{i}", [128, DM], BF16))
                     for i in range(2)]
        fb["ps_h"] = [ph.enter_context(nc.psum_tensor(f"{px}ps_h{i}", [128, 512], F32))
                      for i in range(2)]
        fb["ps_y"] = [ph.enter_context(nc.psum_tensor(f"{px}ps_y{i}", [128, DM], F32))
                      for i in range(2)]
        fb["pst_free"] = [0, 0]
        fb["xnbf_free"] = [0, 0]
        fb["h_act"] = [0, 0]
        fb["hsq_done"] = [0, 0]
        fb["y_free"] = (S_ACT, 0)
        fb["slot_free"] = [[], [], []]
        fb["x1o_free"] = 0
        return fb

    def emit_ffn_phase(sc, fb, x_src_dram, y_dst_dram, n_tiles, w_wfc, w_wpj,
                       pre_fns=None):
        wfc3 = fb["wfc"][:].rearrange("p (c f) -> p c f", c=DC)
        wpj3 = fb["wpj"][:].rearrange("p (c f) -> p c f", c=FC)
        xnfm2 = [fb["xnfm"][i][:].rearrange("p (c t) -> p c t", c=DC)
                 for i in range(2)]
        n_st = n_tiles // 2
        xn_ready = [[], [], []]
        st_ready = [None, None]
        ln_done = [None, None]

        def u_load(st, j):
            sl = st % 3
            if j == 0:
                xn_ready[sl] = []
            x_tile = fb["xst"][sl][:, j * DM:(j + 1) * DM]
            if pre_fns is None:
                tile = st * 2 + j
                sc.add("sync", lambda e, tile=tile, x_tile=x_tile: e.dma_start(
                    out=x_tile, in_=x_src_dram[tile * 128:(tile + 1) * 128, :]),
                    waits=list(fb["slot_free"][sl]), incs=[(XLD[sl], 16)])
            else:
                pre_fns[0](sc, st, j, fb["slot_free"][sl])

        def u_preln(st, j):
            sl = st % 3
            x_tile = fb["xst"][sl][:, j * DM:(j + 1) * DM]
            if pre_fns is None:
                st_ready[j] = [XLD[sl].at()]
            else:
                st_ready[j] = pre_fns[1](sc, st, j, x_tile)

        def u_ffnln(st, j):
            sl = st % 3
            x_tile = fb["xst"][sl][:, j * DM:(j + 1) * DM]
            emit_ln(sc, x_tile, fb["xnbf"][j][:], fb["stats"], (st * 2 + j) % 8,
                    st_ready[j] + [(S_PE, fb["xnbf_free"][j])])
            ln_done[j] = S_DVE.at()

        def u_trans(st, j):
            xnfm = xnfm2[st % 2]
            emit_transpose_tile(
                sc, lambda c, j=j: fb["xnbf"][j][:, c * 128:(c + 1) * 128],
                xnfm[:, :, j * 128:(j + 1) * 128], fb["pst"][j][:],
                [ln_done[j], (S_DVE, fb["pst_free"][j])])
            fb["pst_free"][j] = S_DVE.n
            fb["xnbf_free"][j] = S_PE.n
            xn_ready[st % 3].append(S_DVE.at())

        def stage1_micro(st):
            return [lambda: u_load(st, 0), lambda: u_load(st, 1),
                    lambda: u_preln(st, 0), lambda: u_preln(st, 1),
                    lambda: u_ffnln(st, 0), lambda: u_ffnln(st, 1),
                    lambda: u_trans(st, 0), lambda: u_trans(st, 1)]

        PLAN_F = [1, 3, 5, 11, 17, 22, 27, 29]

        def emit_mm1(f, xnfm, xr):
            hp = f % 2
            ps_h = fb["ps_h"][hp]
            hrelu = fb["hrelu"][hp]
            hsq = fb["hsq"][hp]

            def mm1(e, f=f, ps_h=ps_h, xnfm=xnfm):
                ins = None
                for d in range(DC):
                    ins = e.matmul(ps_h[:, 0:256],
                                   wfc3[:, d, f * 128:(f + 1) * 128],
                                   xnfm[:, d, :], start=(d == 0),
                                   stop=(d == DC - 1))
                return ins
            sc.add("tensor", mm1,
                   waits=list(xr) + [(WFC[f // 16], w_wfc[f // 16]),
                                     (S_ACT, fb["h_act"][hp])],
                   incs=[(S_PE, 1)])
            sc.add("scalar", lambda e, hrelu=hrelu, ps_h=ps_h: e.activation(
                hrelu[:], ps_h[:, 0:256], AF.Relu),
                waits=[S_PE.at()], incs=[(S_ACT, 1)])
            fb["h_act"][hp] = S_ACT.n
            sc.add("vector", lambda e, hsq=hsq, hrelu=hrelu: e.tensor_mul(
                hsq[:], hrelu[:], hrelu[:]),
                waits=[S_ACT.at()], incs=[(S_DVE, 1)])
            fb["hsq_done"][hp] = S_DVE.n

        def emit_mm2(f):
            hp = f % 2
            hsq = fb["hsq"][hp]
            w_mm2 = [(S_DVE, fb["hsq_done"][hp]),
                     (WPJ[f // 16], w_wpj[f // 16])]
            if f == 0:
                w_mm2.append(fb["y_free"])

            def mm2(e, f=f, hsq=hsq):
                ins = None
                for ts in range(2):
                    for dh in range(2):
                        ins = e.matmul(
                            fb["ps_y"][ts][:, dh * 512:(dh + 1) * 512],
                            hsq[:, ts * 128:(ts + 1) * 128],
                            wpj3[:, f, dh * 512:(dh + 1) * 512],
                            start=(f == 0), stop=(f == FC - 1))
                return ins
            sc.add("tensor", mm2, waits=w_mm2, incs=[(S_PE, 1)])

        def stage2(st, micro, plan_f):
            sl = st % 3
            hp2 = st % 2
            xst = fb["xst"][sl]
            x1o = fb["x1o"][hp2]
            xr = xn_ready[sl]
            mi = 0
            for f in range(FC):
                emit_mm1(f, xnfm2[st % 2], xr)
                if f >= 1:
                    emit_mm2(f - 1)
                while mi < len(micro) and f >= plan_f[mi]:
                    micro[mi]()
                    mi += 1
            while mi < len(micro):
                micro[mi]()
                mi += 1
            emit_mm2(FC - 1)
            mm2_done = S_PE.at()
            evac_at = emit_stair_evac(sc, [fb["ps_y"][0][:], fb["ps_y"][1][:]],
                                      fb["stair"], [mm2_done])
            fb["y_free"] = S_ACT.at()
            dve_steps = stair_dve_steps(
                sc, xst[:, 0:2 * DM], x1o[:, 0:2 * DM], fb["stair"], evac_at,
                [(X1O, fb["x1o_free"])])

            def ystore():
                sc.add("scalar", lambda e, st=st, x1o=x1o: e.dma_start(
                    out=y_dst_dram[st * 256:(st + 1) * 256, :].rearrange(
                        "(j p) d -> p j d", p=128),
                    in_=x1o[:, 0:2 * DM].rearrange("p (j d) -> p j d", j=2)),
                    waits=[S_DVE.at()], incs=[(X1O, 16)], dma=True)
                fb["slot_free"][sl] = [S_DVE.at(), S_ACT.at()]
                fb["x1o_free"] = X1O.n
            return dve_steps + [ystore]

        TAIL_F = [0, 0, 1, 2, 3, 4, 5]
        for s1 in stage1_micro(0):
            s1()
        tail = []
        for it in range(1, n_st + 1):
            micro = list(tail)
            plan = TAIL_F[:len(tail)]
            if it < n_st:
                micro += stage1_micro(it)
                plan = plan + [f + 6 for f in PLAN_F]
            tail = stage2(it - 1, micro, plan)
        for s in tail:
            s()

    with nc.Block() as blk:
        sc = Sched({'vector': S_DVE, 'scalar': S_ACT, 'gpsimd': S_GP})
        with ExitStack() as ph:
            wfc_sb = ph.enter_context(nc.sbuf_tensor("wfcA", [128, DC * FF], BF16))
            wpj_sb = ph.enter_context(nc.sbuf_tensor("wpjA", [128, FC * DM], BF16))
            load_consts(sc)
            load_ffn_weights(sc, wfc_sb, wpj_sb)
            fb = alloc_ffn_bufs(ph, wfc_sb, wpj_sb, "a")
            emit_ffn_phase(sc, fb, x_in, x1_dram, NT,
                           (WFC[0].n, WFC[1].n), (WPJ[0].n, WPJ[1].n))
            phase_end(sc)
            sc.emit(blk)

    def debug_dump(srcs):
        with nc.Block() as blk:
            sc = Sched({'vector': S_DVE, 'scalar': S_ACT, 'gpsimd': S_GP})
            with ExitStack() as ph:
                buf = ph.enter_context(nc.sbuf_tensor("dbg", [128, DM], F32))
                n = 0
                for (src, rows, w) in srcs:
                    for t in range(rows // 128):
                        if n >= NT:
                            break
                        sc.add("gpsimd", lambda e, src=src, t=t, w=w: e.dma_start(
                            out=buf[:, 0:w], in_=src[t * 128:(t + 1) * 128, 0:w]),
                            waits=[CONG.at()], incs=[(CONG, 16)], dma=True)
                        sc.add("gpsimd", lambda e, n=n, w=w: e.dma_start(
                            out=y_out[n * 128:(n + 1) * 128, 0:w], in_=buf[:, 0:w]),
                            waits=[CONG.at()], incs=[(CONG, 16)], dma=True)
                        n += 1
                phase_end(sc)
                sc.emit(blk)
        es.close()
        return nc

    if upto == "A":
        return debug_dump([(x1_dram, LLOC, DM)])

    with nc.Block() as blk:
        sc = Sched({'vector': S_DVE, 'scalar': S_ACT, 'gpsimd': S_GP})
        with ExitStack() as ph:
            wqkv_sb = ph.enter_context(
                nc.sbuf_tensor("wqkvB", [128, DC * 3 * DM], BF16))
            wq3 = wqkv_sb[:].rearrange("p (c f) -> p c f", c=DC)
            wq_v = wqkv.rearrange("(c p) f -> p c f", p=128)
            for half in range(2):
                fs = slice(half * (3 * DM // 2), (half + 1) * (3 * DM // 2))
                sc.add("gpsimd", lambda e, fs=fs: e.dma_start(
                    out=wq3[:, :, fs], in_=wq_v[:, :, fs]), incs=[(WQKV, 16)], dma=True)
            w_ld = WQKV.n

            bhat_all = ph.enter_context(nc.sbuf_tensor("bhat_all", [128, NT], F32))
            vscale_all = ph.enter_context(nc.sbuf_tensor("vscale_all", [128, NT], F32))
            ssm_all = ph.enter_context(nc.sbuf_tensor("ssm_all", [128, NT], F32))
            vs_tmp = ph.enter_context(nc.sbuf_tensor("vs_tmp", [128, NT], F32))
            id_f = ph.enter_context(nc.sbuf_tensor("id_f", [128, 128], F32))
            two_id = ph.enter_context(nc.sbuf_tensor("two_id", [128, 128], F32))
            masksu = ph.enter_context(nc.sbuf_tensor("masksu_s", [128, 128], F32))
            maskiu = ph.enter_context(nc.sbuf_tensor("maskiu_s", [128, 128], F32))
            for (dst, srcc) in ((id_f, ident_in), (two_id, twoid_in),
                                (masksu, masksu_in), (maskiu, maskiu_in)):
                sc.add("sync", lambda e, dst=dst, srcc=srcc: e.dma_start(
                    out=dst[:], in_=srcc), incs=[(CONS, 16)])
            xst = [ph.enter_context(nc.sbuf_tensor(f"bx{i}", [128, 2 * DM], F32))
                   for i in range(2)]
            xnbf = [ph.enter_context(nc.sbuf_tensor(f"bxn{i}", [128, DM], BF16))
                    for i in range(2)]
            xnfm_t = ph.enter_context(nc.sbuf_tensor("bxnfm", [128, DC * 256], BF16))
            xnfm = xnfm_t[:].rearrange("p (c t) -> p c t", c=DC)
            qfst_t = ph.enter_context(nc.sbuf_tensor("bqf", [128, DC * 256], BF16))
            qfst = qfst_t[:].rearrange("p (c t) -> p c t", c=DC)
            kfst_t = ph.enter_context(nc.sbuf_tensor("bkf", [128, DC * 256], BF16))
            kfst = kfst_t[:].rearrange("p (c t) -> p c t", c=DC)
            ktbf = [ph.enter_context(nc.sbuf_tensor(f"bkt{i}", [128, DM], BF16))
                    for i in range(2)]
            vtbf = [ph.enter_context(nc.sbuf_tensor(f"bvt{i}", [128, DM], BF16))
                    for i in range(2)]
            stats = {
                "st6": ph.enter_context(nc.sbuf_tensor("bst6", [128, 48], F32)),
                "st2": ph.enter_context(nc.sbuf_tensor("bst2", [128, 16], F32)),
                "rstd": ph.enter_context(nc.sbuf_tensor("brstd", [128, 8], F32)),
                "lnt": ph.enter_context(nc.sbuf_tensor("blnt", [128, 8], F32)),
                "junk": ph.enter_context(nc.sbuf_tensor("bst_junk", [128, DM], F32)),
            }
            kss = ph.enter_context(nc.sbuf_tensor("bkss", [128, 8], F32))
            betas = ph.enter_context(nc.sbuf_tensor("bbetas", [128, 8], F32))
            rss = ph.enter_context(nc.sbuf_tensor("brss", [128, 8], F32))
            e_f = ph.enter_context(nc.sbuf_tensor("be_f", [128, 128], F32))
            mt_b = ph.enter_context(nc.sbuf_tensor("bmt", [128, 128], BF16))
            tm_b = ph.enter_context(nc.sbuf_tensor("btm", [128, 128], BF16))
            x_nb = [ph.enter_context(nc.sbuf_tensor(f"bxn2_{i}", [128, 128], BF16))
                    for i in range(2)]
            tx_b = ph.enter_context(nc.sbuf_tensor("btx", [128, 128], BF16))
            d_b = ph.enter_context(nc.sbuf_tensor("bd", [128, 128], BF16))
            xl_b = ph.enter_context(nc.sbuf_tensor("bxl", [128, 256], BF16))
            pst = ph.enter_context(nc.psum_tensor("bpst", [128, DM], BF16))
            ps_n = ph.enter_context(nc.psum_tensor("bpsn", [128, 512], F32))
            ps_q = [ph.enter_context(nc.psum_tensor(f"bpsq{i}", [128, 512], F32))
                    for i in range(2)]
            ps_kv = [ph.enter_context(nc.psum_tensor(f"bpskv{i}", [128, DM], F32))
                     for i in range(2)]

            pst_free = 0
            xnbf_free = [0, 0]
            q_evac = [0, 0]
            kv_evac = [0, 0]
            kv_dve = [0, 0]
            beta_done = 0
            slot_free = [[], []]
            for st in range(NT // 2):
                sl = st % 2
                xn_ready = []
                for j in range(2):
                    tile = st * 2 + j
                    x_tile = xst[sl][:, j * DM:(j + 1) * DM]
                    sc.add("sync", lambda e, tile=tile, x_tile=x_tile: e.dma_start(
                        out=x_tile, in_=x1_dram[tile * 128:(tile + 1) * 128, :]),
                        waits=list(slot_free[sl]), incs=[(XLD[sl], 16)])
                for j in range(2):
                    tile = st * 2 + j
                    x_tile = xst[sl][:, j * DM:(j + 1) * DM]
                    emit_ln(sc, x_tile, xnbf[j][:], stats, tile % 8,
                            [XLD[sl].at(), (S_PE, xnbf_free[j])])
                    ln_done = S_DVE.at()
                    emit_transpose_tile(
                        sc, lambda c, j=j: xnbf[j][:, c * 128:(c + 1) * 128],
                        xnfm[:, :, j * 128:(j + 1) * 128], pst[:],
                        [ln_done, (S_DVE, pst_free)])
                    pst_free = S_DVE.n
                    xnbf_free[j] = S_PE.n
                    xn_ready.append(S_DVE.at())
                slot_free[sl] = [S_DVE.at(), S_ACT.at()]
                for dco in range(DC):
                    qp = dco % 2

                    def qmm(e, dco=dco, qp=qp):
                        ins = None
                        for d in range(DC):
                            ins = e.matmul(ps_q[qp][:, 0:256],
                                           wq3[:, d, dco * 128:(dco + 1) * 128],
                                           xnfm[:, d, :], start=(d == 0),
                                           stop=(d == DC - 1))
                        return ins
                    sc.add("tensor", qmm,
                           waits=list(xn_ready) + [(WQKV, w_ld),
                                                   (S_DVE, q_evac[qp]),
                                                   (S_ACT, beta_done)],
                           incs=[(S_PE, 1)])
                    sc.add("vector", lambda e, dco=dco, qp=qp: e.tensor_scalar_mul(
                        qfst[:, dco, :], ps_q[qp][:, 0:256], 1.0 / 32.0),
                        waits=[S_PE.at(), QST.at()], incs=[(S_DVE, 1)])
                    q_evac[qp] = S_DVE.n
                sc.add("scalar", lambda e, st=st: e.dma_start(
                    out=qfw[st // 4][:, :, (st % 4) * 256:(st % 4 + 1) * 256],
                    in_=qfst_t[:].rearrange("p (c t) -> p c t", c=DC)),
                    waits=[S_DVE.at()], incs=[(QST, 16)], dma=True)
                for j in range(2):
                    tile = st * 2 + j

                    def kmm(e, j=j):
                        ins = None
                        for dh in range(2):
                            for d in range(DC):
                                ins = e.matmul(
                                    ps_kv[j][:, dh * 512:(dh + 1) * 512],
                                    xnfm[:, d, j * 128:(j + 1) * 128],
                                    wq3[:, d, DM + dh * 512:DM + (dh + 1) * 512],
                                    start=(d == 0), stop=(d == DC - 1))
                        return ins
                    sc.add("tensor", kmm,
                           waits=list(xn_ready) + [(S_ACT, kv_evac[j]),
                                                   (S_DVE, kv_dve[j])],
                           incs=[(S_PE, 1)])
                    kmm_done = S_PE.at()
                    sc.add("vector", lambda e, j=j: e.tensor_copy(
                        ktbf[j][:], ps_kv[j][:]),
                        waits=[kmm_done, KTST[j].at()], incs=[(S_DVE, 1)])
                    kt_done = S_DVE.at()
                    kv_dve[j] = S_DVE.n
                    sc.add("scalar", lambda e, j=j, tile=tile: e.activation(
                        stats["junk"][:], ps_kv[j][:], AF.Square,
                        accum_out=kss[:, tile % 8:tile % 8 + 1]),
                        waits=[kmm_done, kt_done], incs=[(S_ACT, 1)])
                    kv_evac[j] = S_ACT.n
                    g = tile // 8
                    r0 = g * HBLK + (tile % 8) * 128
                    sc.add("scalar", lambda e, j=j, r0=r0: e.dma_start(
                        out=cc1v[r0:r0 + 128, :], in_=ktbf[j][:]),
                        waits=[kt_done], incs=[(KTST[j], 16)], dma=True)
                    emit_transpose_tile(
                        sc, lambda c, j=j: ktbf[j][:, c * 128:(c + 1) * 128],
                        kfst[:, :, j * 128:(j + 1) * 128], pst[:],
                        [kt_done, (S_DVE, pst_free), KFST.at()])
                    pst_free = S_DVE.n
                sc.add("scalar", lambda e, st=st: e.dma_start(
                    out=kfw[st // 4][:, :, (st % 4) * 256:(st % 4 + 1) * 256],
                    in_=kfst_t[:].rearrange("p (c t) -> p c t", c=DC)),
                    waits=[S_DVE.at()], incs=[(KFST, 16)], dma=True)
                for j in range(2):
                    tile = st * 2 + j

                    def vmm(e, j=j):
                        ins = None
                        for dh in range(2):
                            for d in range(DC):
                                ins = e.matmul(
                                    ps_kv[j][:, dh * 512:(dh + 1) * 512],
                                    xnfm[:, d, j * 128:(j + 1) * 128],
                                    wq3[:, d, 2 * DM + dh * 512:2 * DM + (dh + 1) * 512],
                                    start=(d == 0), stop=(d == DC - 1))
                        return ins
                    sc.add("tensor", vmm,
                           waits=[(S_ACT, kv_evac[j]), (S_DVE, kv_dve[j])],
                           incs=[(S_PE, 1)])
                    sc.add("scalar", lambda e, j=j: e.activation(
                        vtbf[j][:], ps_kv[j][:], AF.Silu),
                        waits=[S_PE.at(), VTST[j].at()], incs=[(S_ACT, 1)])
                    kv_evac[j] = S_ACT.n
                    for dh in range(2):
                        rv = (tile // 8) * 2048 + dh * 1024 + (tile % 8) * 128
                        sc.add("scalar", lambda e, j=j, rv=rv, dh=dh: e.dma_start(
                            out=ccv_in.ap()[rv:rv + 128, :],
                            in_=vtbf[j][:, dh * 512:(dh + 1) * 512]),
                            waits=[S_ACT.at()], incs=[(VTST[j], 16)], dma=True)
                for j in range(2):
                    tile = st * 2 + j
                    s8 = slice(tile % 8, tile % 8 + 1)
                    ci = slice(tile, tile + 1)

                    def bmm(e, j=j):
                        ins = None
                        for d in range(DC):
                            ins = e.matmul(ps_q[0][:, 256 + j:256 + j + 1],
                                           xnfm[:, d, j * 128:(j + 1) * 128],
                                           wb_sb[:, d:d + 1],
                                           start=(d == 0), stop=(d == DC - 1))
                        return ins
                    sc.add("tensor", bmm,
                           waits=[(S_DVE, q_evac[0]), (CONG, 128),
                                  (S_ACT, beta_done)],
                           incs=[(S_PE, 1)])
                    sc.add("scalar", lambda e, j=j, s8=s8: e.activation(
                        betas[:, s8], ps_q[0][:, 256 + j:256 + j + 1], AF.Sigmoid),
                        waits=[S_PE.at()], incs=[(S_ACT, 1)])
                    beta_done = S_ACT.n
                    sc.add("vector", lambda e, s8=s8, ci=ci: e.tensor_single_scalar(
                        ssm_all[:, ci], kss[:, s8], 1e-24, ALU.max),
                        waits=[(S_ACT, kv_evac[j])], incs=[(S_DVE, 1)])
                    sc.add("vector", lambda e, s8=s8, ci=ci: e.reciprocal(
                        rss[:, s8], ssm_all[:, ci]),
                        waits=[S_DVE.at()], incs=[(S_DVE, 1)])
                    sc.add("vector", lambda e, s8=s8, ci=ci: e.tensor_mul(
                        bhat_all[:, ci], betas[:, s8], rss[:, s8]),
                        waits=[S_DVE.at(), (S_ACT, beta_done)], incs=[(S_DVE, 1)])
                for j in range(2):
                    tile = st * 2 + j
                    bh = bhat_all[:, tile:tile + 1]

                    def gmm(e, j=j):
                        ins = None
                        for d in range(DC):
                            kfs = kfst[:, d, j * 128:(j + 1) * 128]
                            ins = e.matmul(ps_n[:, 0:128], kfs, kfs,
                                           start=(d == 0), stop=(d == DC - 1))
                        return ins
                    sc.add("tensor", gmm, waits=[(S_DVE, S_DVE.n), (CONS, 64)],
                           incs=[(S_PE, 1)])
                    sc.add("vector", lambda e, bh=bh: e.scalar_tensor_tensor(
                        e_f[:], ps_n[:, 0:128], bh, masksu[:], ALU.mult, ALU.mult),
                        waits=[S_PE.at()], incs=[(S_DVE, 1)])
                    sc.add("vector", lambda e: e.tensor_sub(x_nb[0][:], id_f[:],
                                                            e_f[:]))
                    sc.add("vector", lambda e: e.tensor_add(mt_b[:], id_f[:],
                                                            e_f[:]),
                           incs=[(S_DVE, 1)])
                    sc.add("tensor", lambda e: e.transpose(pst[:, 0:128], mt_b[:],
                                                           id_b[:]),
                           waits=[(S_DVE, S_DVE.n), (S_DVE, pst_free)],
                           incs=[(S_PE, 1)])
                    sc.add("vector", lambda e: e.tensor_copy(tm_b[:], pst[:, 0:128]),
                           waits=[S_PE.at()], incs=[(S_DVE, 1)])
                    pst_free = S_DVE.n

                    def lmm(e, j=j):
                        ins = None
                        for d in range(DC):
                            ins = e.matmul(ps_n[:, 128:256],
                                           kfst[:, d, j * 128:(j + 1) * 128],
                                           qfst[:, d, j * 128:(j + 1) * 128],
                                           start=(d == 0), stop=(d == DC - 1))
                        return ins
                    sc.add("tensor", lmm, waits=[(S_DVE, S_DVE.n)], incs=[(S_PE, 1)])
                    sc.add("vector", lambda e: e.tensor_mul(xl_b[:, 128:256],
                                                            ps_n[:, 128:256],
                                                            maskiu[:]),
                           waits=[S_PE.at(), XLST[0].at(), XLST[1].at()],
                           incs=[(S_DVE, 1)])
                    xc = 0
                    for it in range(6):
                        sc.add("tensor", lambda e, xc=xc: e.matmul(
                            ps_n[:, 256:384], tm_b[:], x_nb[xc][:],
                            start=True, stop=True),
                            waits=[(S_DVE, S_DVE.n)], incs=[(S_PE, 1)])
                        sc.add("vector", lambda e: e.tensor_sub(
                            d_b[:], two_id[:], ps_n[:, 256:384]),
                            waits=[S_PE.at()], incs=[(S_DVE, 1)])
                        sc.add("tensor", lambda e, xc=xc: e.transpose(
                            pst[:, 128:256], x_nb[xc][:], id_b[:]),
                            incs=[(S_PE, 1)])
                        sc.add("vector", lambda e: e.tensor_copy(
                            tx_b[:], pst[:, 128:256]),
                            waits=[S_PE.at()], incs=[(S_DVE, 1)])
                        sc.add("tensor", lambda e: e.matmul(
                            ps_n[:, 384:512], tx_b[:], d_b[:],
                            start=True, stop=True),
                            waits=[(S_DVE, S_DVE.n)], incs=[(S_PE, 1)])
                        dst = xl_b[:, 0:128] if it == 5 else x_nb[1 - xc][:]
                        sc.add("vector", lambda e, dst=dst: e.tensor_copy(
                            dst, ps_n[:, 384:512]),
                            waits=[S_PE.at()], incs=[(S_DVE, 1)])
                        xc = 1 - xc
                    rxl = (tile // 8) * HBLK + 3072 + ((tile % 8) // 4) * 128
                    cxl = ((tile % 8) % 4) * 256
                    sc.add("scalar", lambda e, rxl=rxl, cxl=cxl: e.dma_start(
                        out=cc1v[rxl:rxl + 128, cxl:cxl + 256], in_=xl_b[:]),
                        waits=[(S_DVE, S_DVE.n)], incs=[(XLST[j], 16)], dma=True)
                if st == 3 or st == 7:
                    g = st // 4
                    hs = slice(g * 8, g * 8 + 8)
                    sc.add("scalar", lambda e, hs=hs: e.activation(
                        vs_tmp[:, hs], ssm_all[:, hs], AF.Ln),
                        waits=[(S_DVE, S_DVE.n)], incs=[(S_ACT, 1)])
                    sc.add("scalar", lambda e, hs=hs: e.activation(
                        vscale_all[:, hs], vs_tmp[:, hs], AF.Exp, scale=0.5),
                        waits=[S_ACT.at()], incs=[(S_ACT, 1)])
                    sc.add("scalar", lambda e, g=g, hs=hs: e.dma_start(
                        out=cc1c_in.ap()[g * 128:(g + 1) * 128, 0:8],
                        in_=bhat_all[:, hs]),
                        waits=[(S_DVE, S_DVE.n)], incs=[(QST, 16)], dma=True)
                    sc.add("scalar", lambda e, g=g, hs=hs: e.dma_start(
                        out=cc1c_in.ap()[g * 128:(g + 1) * 128, 8:16],
                        in_=vscale_all[:, hs]),
                        waits=[S_ACT.at()], incs=[(QST, 16)], dma=True)
                    sc.add("scalar", lambda e: e.nop(),
                           waits=[QST.at(), KFST.at(), KTST[0].at(), KTST[1].at(),
                                  VTST[0].at(), VTST[1].at(), XLST[0].at(),
                                  XLST[1].at()],
                           incs=[(BRDY, 1)], dma=True)
                    sc.add("gpsimd", lambda e, g=g: e.collective_compute(
                        "AllGather", ALU.bypass, replica_groups=[list(range(8))],
                        ins=[cc1_in.ap()[g * HBLK:(g + 1) * HBLK, :].opt()],
                        outs=[cc1_out.ap()[g * 8 * HBLK:(g + 1) * 8 * HBLK, :].opt()]),
                        waits=[(BRDY, g + 1)], incs=[(CCA, 1)], dma=True)
                    sc.add("gpsimd", lambda e, g=g: e.collective_compute(
                        "AllGather", ALU.bypass, replica_groups=[list(range(8))],
                        ins=[ccv_in.ap()[g * 2048:(g + 1) * 2048, :].opt()],
                        outs=[ccv_out.ap()[g * 8 * 2048:(g + 1) * 8 * 2048, :].opt()]),
                        incs=[(CCB, 1)], dma=True)
                    sc.add("gpsimd", lambda e, g=g: e.collective_compute(
                        "AllGather", ALU.bypass, replica_groups=[list(range(8))],
                        ins=[cc1c_in.ap()[g * 128:(g + 1) * 128, :].opt()],
                        outs=[cc1c_out.ap()[g * 1024:(g + 1) * 1024, :].opt()]),
                        incs=[(CC, 1)], dma=True)
            phase_end(sc, exclude=(CC, CCA, CCB))
            sc.emit(blk)

    if upto == "B":
        return debug_dump([(cc1v, LLOC, 1024)])

    with nc.Block() as blk:
        sc = Sched({'vector': S_DVE, 'scalar': S_ACT, 'gpsimd': S_GP})
        with ExitStack() as ph:
            sbq = [ph.enter_context(nc.sbuf_tensor(f"csbq{i}", [128, DC * 1024], BF16))
                   for i in range(2)]
            sbk = [ph.enter_context(nc.sbuf_tensor(f"csbk{i}", [128, DC * 1024], BF16))
                   for i in range(2)]
            kt_i = [ph.enter_context(nc.sbuf_tensor(f"ckt{i}", [128, DM], BF16))
                    for i in range(2)]
            v_i = [ph.enter_context(nc.sbuf_tensor(f"cv{i}", [128, DV], BF16))
                   for i in range(2)]
            xl_i = [ph.enter_context(nc.sbuf_tensor(f"cxl{i}", [128, 256], BF16))
                    for i in range(2)]
            r_b = ph.enter_context(nc.sbuf_tensor("cr", [128, DV], BF16))
            u_b = ph.enter_context(nc.sbuf_tensor("cu", [128, DV], BF16))
            o_sb = [ph.enter_context(nc.sbuf_tensor(f"co{i}", [128, DV], BF16))
                    for i in range(2)]
            S_f = ph.enter_context(nc.sbuf_tensor("cSf", [128, DC * DV], F32))
            S_bf = ph.enter_context(nc.sbuf_tensor("cSb", [128, DC * DV], BF16))
            Sf3 = S_f[:].rearrange("p (c f) -> p c f", c=DC)
            Sb3 = S_bf[:].rearrange("p (c f) -> p c f", c=DC)
            ps_r = ph.enter_context(nc.psum_tensor("cpsr", [128, DV], F32))
            ps_k2 = ph.enter_context(nc.psum_tensor("cpsk2", [128, DV], F32))
            ps_o = ph.enter_context(nc.psum_tensor("cpso", [128, DV], F32))

            def emit_bh2(blkhalf):
                sc.add("gpsimd", lambda e, blkhalf=blkhalf: e.indirect_dma_start(
                    bh2[:, blkhalf * 16:(blkhalf + 1) * 16], None, cc1c_out.ap(),
                    IndirectOffsetOnAxis(idxbh[:, blkhalf:blkhalf + 1], 0)),
                    waits=[(CC, 1 if blkhalf % 2 == 0 else 2), (CONG, 128)],
                    incs=[(BH2S[blkhalf], 16)], dma=True)

            emit_bh2(0)
            emit_bh2(2)

            def emit_stage_gathers(s, guard_val=None):
                sb = s % 2
                ccw = 1 if s < 2 else 2
                guard = [(S_PE, guard_val)] if guard_val is not None else []
                for reg in range(2):
                    dst = sbq[sb] if reg == 0 else sbk[sb]
                    for d in range(DC):
                        icol = s * 16 + reg * 8 + d
                        sc.add("gpsimd", lambda e, dst=dst, d=d, icol=icol:
                               e.indirect_dma_start(
                                   dst[:, d * 1024:(d + 1) * 1024], None,
                                   cc1_out.ap(),
                                   IndirectOffsetOnAxis(idxfm[:, icol:icol + 1], 0)),
                               waits=[(CC, ccw), (CONG, 128)] + guard,
                               incs=[(SQG[sb], 16)], dma=True)

            chunk_pe_done = [0, 0]

            def emit_chunk_loads(i):
                cb = i % 2
                ccw = 1 if i < 16 else 2
                sc.add("gpsimd", lambda e, i=i, cb=cb: e.indirect_dma_start(
                    kt_i[cb][:], None, cc1_out.ap(),
                    IndirectOffsetOnAxis(idxktv[:, i:i + 1], 0)),
                    waits=[(S_PE, chunk_pe_done[cb]), (CC, ccw), (CONG, 128)],
                    incs=[(CHT[cb], 16)], dma=True)
                sc.add("gpsimd", lambda e, i=i, cb=cb: e.indirect_dma_start(
                    xl_i[cb][:], None, cc1_out.ap(),
                    IndirectOffsetOnAxis(idxxl[:, i:i + 1], 0),
                    element_offset=(i % 4) * 256),
                    waits=[(CONG, 128)], incs=[(CHT[cb], 16)], dma=True)
                sc.add("gpsimd", lambda e, i=i, cb=cb: e.indirect_dma_start(
                    v_i[cb][:], None, ccv_out.ap(),
                    IndirectOffsetOnAxis(idxv[:, i:i + 1], 0)),
                    waits=[(CC, ccw), (CONG, 128)],
                    incs=[(CHT[cb], 16)], dma=True)

            emit_chunk_loads(0)
            emit_chunk_loads(1)
            emit_stage_gathers(0)
            emit_stage_gathers(1)
            s_cast_done = 0
            r_evac = 0
            o_evac = 0
            stage_guard = [None, None]
            for i in range(NCHUNK):
                if i == 12 or i == 16:
                    stage_guard[0 if i == 12 else 1] = S_PE.n
                cb = i % 2
                sb = (i // 4) % 2
                j = ((i // 8) % 2) * 4 + i % 4
                qf3 = sbq[sb][:].rearrange("p (c t) -> p c t", c=DC)
                kf3 = sbk[sb][:].rearrange("p (c t) -> p c t", c=DC)
                sqw = (SQG[sb], 256 * (i // 16 + 1))
                if i >= 2:
                    emit_chunk_loads(i)
                if i == 13:
                    emit_stage_gathers(2, stage_guard[0])
                if i == 15:
                    emit_bh2(1)
                    emit_bh2(3)
                if i == 17:
                    emit_stage_gathers(3, stage_guard[1])
                ld_w = [sqw, CHT[cb].at()]
                bhcol = ((i // 4) % 2) * 2 + (i // 16)
                bh = bh2[:, bhcol * 16 + j:bhcol * 16 + j + 1]
                vs = bh2[:, bhcol * 16 + 8 + j:bhcol * 16 + 8 + j + 1]

                bhw = (BH2S[bhcol], 16)
                if i > 0:
                    def ks0(e, kf3=kf3, j=j):
                        ins = None
                        for d in range(DC):
                            ins = e.matmul(ps_r[:], kf3[:, d, j * 128:(j + 1) * 128],
                                           Sb3[:, d, :],
                                           start=(d == 0), stop=(d == DC - 1))
                        return ins
                    sc.add("tensor", ks0, waits=ld_w + [(S_ACT, s_cast_done),
                                                        (S_DVE, r_evac)],
                           incs=[(S_PE, 1)])
                    sc.add("vector", lambda e, cb=cb, vs=vs: e.scalar_tensor_tensor(
                        r_b[:], v_i[cb][:], vs, ps_r[:], ALU.mult, ALU.subtract),
                        waits=[S_PE.at(), (CONG, 128), bhw], incs=[(S_DVE, 1)])
                else:
                    sc.add("vector", lambda e, cb=cb, vs=vs: e.tensor_scalar(
                        r_b[:], v_i[cb][:], vs, None, ALU.mult),
                        waits=ld_w + [(CONG, 128), bhw], incs=[(S_DVE, 1)])
                r_evac = S_DVE.n
                if i > 0:
                    def qs0(e, qf3=qf3, j=j):
                        ins = None
                        for d in range(DC):
                            ins = e.matmul(ps_o[:], qf3[:, d, j * 128:(j + 1) * 128],
                                           Sb3[:, d, :],
                                           start=(d == 0), stop=False)
                        return ins
                    sc.add("tensor", qs0, waits=ld_w + [(S_ACT, s_cast_done),
                                                        (S_DVE, o_evac)],
                           incs=[(S_PE, 1)])
                sc.add("tensor", lambda e, cb=cb: e.matmul(
                    ps_r[:], xl_i[cb][:, 0:128], r_b[:], start=True, stop=True),
                    waits=[(S_DVE, r_evac)], incs=[(S_PE, 1)])
                sc.add("vector", lambda e, bh=bh: e.tensor_scalar(
                    u_b[:], ps_r[:], bh, None, ALU.mult),
                    waits=[S_PE.at()], incs=[(S_DVE, 1)])
                u_done = S_DVE.at()
                def lumm(e, cb=cb, first=(i == 0)):
                    return e.matmul(ps_o[:], xl_i[cb][:, 128:256], u_b[:],
                                    start=first, stop=True)
                lu_w = [u_done]
                if i == 0:
                    lu_w.append((S_DVE, o_evac))
                sc.add("tensor", lumm, waits=lu_w, incs=[(S_PE, 1)])
                sc.add("vector", lambda e, cb=cb: e.tensor_copy(o_sb[cb][:], ps_o[:]),
                       waits=[S_PE.at(), OST[cb].at()], incs=[(S_DVE, 1)])
                o_evac = S_DVE.n
                sc.add("scalar", lambda e, i=i, cb=cb: e.dma_start(
                    out=cc2_in.ap()[i * 128:(i + 1) * 128, :], in_=o_sb[cb][:]),
                    waits=[S_DVE.at()], incs=[(OST[cb], 16)], dma=True)
                def kumm_fn(d, ps):
                    def kumm(e, d=d, cb=cb, ps=ps):
                        return e.matmul(ps[:], kt_i[cb][:, d * 128:(d + 1) * 128],
                                        u_b[:], start=True, stop=True)
                    return kumm

                def s_add(d, ps, pe_at):
                    if i == 0:
                        sc.add("vector", lambda e, d=d, ps=ps: e.tensor_copy(
                            Sf3[:, d, :], ps[:]),
                            waits=[pe_at], incs=[(S_DVE, 1)])
                    else:
                        sc.add("vector", lambda e, d=d, ps=ps: e.tensor_add(
                            Sf3[:, d, :], Sf3[:, d, :], ps[:]),
                            waits=[pe_at], incs=[(S_DVE, 1)])
                    sc.add("scalar", lambda e, d=d: e.activation(
                        Sb3[:, d, :], Sf3[:, d, :], AF.Copy),
                        waits=[S_DVE.at()], incs=[(S_ACT, 1)])

                kps = [ps_r, ps_k2]
                pend = []
                for d in range(DC):
                    ps = kps[d % 2]
                    sc.add("tensor", kumm_fn(d, ps),
                           waits=[(S_DVE, S_DVE.n)], incs=[(S_PE, 1)])
                    pend.append((d, ps, S_PE.at()))
                    if len(pend) == 2:
                        dd, pp, pa = pend.pop(0)
                        s_add(dd, pp, pa)
                for (dd, pp, pa) in pend:
                    s_add(dd, pp, pa)
                r_evac = S_DVE.n
                s_cast_done = S_ACT.n
                chunk_pe_done[cb] = S_PE.n
                if i % 8 == 7:
                    pp = i // 8
                    sc.add("scalar", lambda e: e.nop(),
                           waits=[OST[0].at(), OST[1].at()],
                           incs=[(BRDY, 1)], dma=True)
                    sc.add("gpsimd", lambda e, pp=pp: e.collective_compute(
                        "AllGather", ALU.bypass, replica_groups=[list(range(8))],
                        ins=[cc2_in.ap()[pp * 1024:(pp + 1) * 1024, :].opt()],
                        outs=[cc2_out.ap()[pp * 8 * 1024:(pp + 1) * 8 * 1024,
                                           :].opt()]),
                        waits=[(BRDY, 3 + pp)], incs=[(CC, 1)], dma=True)
            phase_end(sc, exclude=(CC, CCA, CCB))
            sc.emit(blk)

    if upto == "C":
        return debug_dump([(cc2_in.ap(), L, DV)])

    with nc.Block() as blk:
        sc = Sched({'vector': S_DVE, 'scalar': S_ACT, 'gpsimd': S_GP})
        with ExitStack() as ph:
            wfc_sb = ph.enter_context(nc.sbuf_tensor("wfcD", [128, DC * FF], BF16))
            wpj_sb = ph.enter_context(nc.sbuf_tensor("wpjD", [128, FC * DM], BF16))
            load_ffn_weights(sc, wfc_sb, wpj_sb)
            fb = alloc_ffn_bufs(ph, wfc_sb, wpj_sb, "d")
            g_sb = ph.enter_context(nc.sbuf_tensor("g_sb", [128, DM], F32))
            b_sb = ph.enter_context(nc.sbuf_tensor("b_sb", [128, DM], F32))
            for (dst, srcc) in ((g_sb, gfull), (b_sb, bfull)):
                sc.add("sync", lambda e, dst=dst, srcc=srcc: e.dma_start(
                    out=dst[:], in_=srcc), incs=[(CONS, 16)])
            x1t = [ph.enter_context(nc.sbuf_tensor(f"dx1_{i}", [128, DM], F32))
                   for i in range(2)]
            obf = [ph.enter_context(nc.sbuf_tensor(f"dob_{i}", [128, DM], BF16))
                   for i in range(2)]
            d_used = [[], []]

            def pre_load(sc, st, j, slot_free):
                tile = st * 2 + j
                guards = list(d_used[j]) + list(slot_free)
                sc.add("sync", lambda e, tile=tile, j=j: e.dma_start(
                    out=x1t[j][:], in_=x1_dram[tile * 128:(tile + 1) * 128, :]),
                    waits=guards, incs=[(DXL[j], 16)])
                for half in range(2):
                    icol = half * 16 + tile
                    sc.add("gpsimd", lambda e, j=j, half=half, icol=icol:
                           e.indirect_dma_start(
                               obf[j][:, half * DV:(half + 1) * DV], None,
                               cc2_out.ap(),
                               IndirectOffsetOnAxis(idxo[:, icol:icol + 1], 0)),
                           waits=guards + [(CC, 3 + tile // 4),
                                           (CONG, CONG.n)],
                           incs=[(DLD[j], 16)], dma=True)

            def pre_ln(sc, st, j, x_tile):
                tile = st * 2 + j
                ld = [DXL[j].at(), DLD[j].at()]
                emit_ln(sc, obf[j][:], x_tile, fb["stats"], tile % 8, ld)
                xn_done = S_DVE.at()
                sc.add("gpsimd", lambda e, x_tile=x_tile: e.tensor_mul(
                    x_tile, x_tile, g_sb[:]),
                    waits=[xn_done, (CONS, 96)])
                sc.add("gpsimd", lambda e, x_tile=x_tile: e.tensor_add(
                    x_tile, x_tile, b_sb[:]))
                sc.add("gpsimd", lambda e, j=j, x_tile=x_tile: e.tensor_add(
                    x_tile, x_tile, x1t[j][:]),
                    waits=[DXL[j].at()], incs=[(S_GP, 1)])
                d_used[j] = [xn_done, S_GP.at(), S_ACT.at()]
                return [S_GP.at()]

            emit_ffn_phase(sc, fb, None, y_out, NT,
                           (WFC[0].n, WFC[1].n), (WPJ[0].n, WPJ[1].n),
                           pre_fns=(pre_load, pre_ln))
            phase_end(sc)
            sc.emit(blk)

    es.close()
    return nc


def prep_in_maps(L0_attn_x, L0_attn_Wq, L0_attn_Wk, L0_attn_Wv, L0_attn_Wb,
                 L0_attn_ln_g, L0_attn_ln_b, L0_ffn_W_fc, L0_ffn_W_proj):
    x = np.ascontiguousarray(np.asarray(L0_attn_x, dtype=np.float32))
    wfcT_h = np.ascontiguousarray(np.asarray(L0_ffn_W_fc, np.float32).T)
    wpjT_h = np.ascontiguousarray(np.asarray(L0_ffn_W_proj, np.float32).T)
    Wq = np.asarray(L0_attn_Wq, np.float32)
    Wk = np.asarray(L0_attn_Wk, np.float32)
    Wv = np.asarray(L0_attn_Wv, np.float32)
    wb_h = np.ascontiguousarray(
        np.asarray(L0_attn_Wb, np.float32).reshape(DC, 128).T)
    gfull_h = np.ascontiguousarray(np.broadcast_to(
        np.asarray(L0_attn_ln_g, np.float32), (128, DM)))
    bfull_h = np.ascontiguousarray(np.broadcast_to(
        np.asarray(L0_attn_ln_b, np.float32), (128, DM)))
    ident_h = np.eye(128, dtype=np.float32)
    twoid_h = 2.0 * np.eye(128, dtype=np.float32)
    masksu_h = np.triu(np.ones((128, 128), np.float32), 1)
    maskiu_h = np.triu(np.ones((128, 128), np.float32), 0)

    p = np.arange(128, dtype=np.int64)
    in_maps = []
    for c in range(8):
        b, h = c // 2, c % 2
        wqkv_h = np.ascontiguousarray(np.concatenate([Wq, Wk, Wv], axis=1))
        HB, G8 = 3328, 8 * 3328
        idxfm = np.zeros((128, 64), np.int64)
        for s in range(4):
            rank, g = 2 * b + s % 2, s // 2
            for reg in range(2):
                for d in range(8):
                    idxfm[:, s * 16 + reg * 8 + d] = (
                        g * G8 + rank * HB + 1024 + reg * 1024 + d * 128 + p)
        idxktv = np.zeros((128, 32), np.int64)
        idxxl = np.zeros((128, 32), np.int64)
        for i in range(32):
            rank, g = 2 * b + (i // 4) % 2, i // 16
            j = ((i // 8) % 2) * 4 + i % 4
            base = g * G8 + rank * HB + j * 128
            idxktv[:, i] = base + p
            idxxl[:, i] = (g * G8 + rank * HB + 3072 + (j // 4) * 128 + p)
        idxbh = np.zeros((128, 4), np.int64)
        for blk in range(2):
            for g in range(2):
                idxbh[:, blk * 2 + g] = g * 1024 + (2 * b + blk) * 128 + p
        idxv = np.zeros((128, 32), np.int64)
        for i in range(32):
            rank, g = 2 * b + (i // 4) % 2, i // 16
            j = ((i // 8) % 2) * 4 + i % 4
            idxv[:, i] = g * 8 * 2048 + rank * 2048 + h * 1024 + j * 128 + p
        idxo = np.zeros((128, 32), np.int64)
        for q in range(2):
            for t in range(NT):
                idxo[:, q * 16 + t] = ((t // 4) * 8 * 1024 + (2 * b + q) * 1024
                                       + h * 512 + (t % 4) * 128 + p)
        in_maps.append({
            "x": np.ascontiguousarray(np.concatenate(
                [x[b, (2 * el + h) * 512:(2 * el + h + 1) * 512]
                 for el in range(4)], axis=0)),
            "wfcT": wfcT_h, "wpjT": wpjT_h, "wqkv": wqkv_h, "wb": wb_h,
            "gfull": gfull_h, "bfull": bfull_h, "ident": ident_h,
            "twoid": twoid_h, "masksu": masksu_h, "maskiu": maskiu_h,
            "idxfm": idxfm.astype(np.int32), "idxktv": idxktv.astype(np.int32),
            "idxbh": idxbh.astype(np.int32), "idxo": idxo.astype(np.int32),
            "idxv": idxv.astype(np.int32), "idxxl": idxxl.astype(np.int32),
        })
    return in_maps


_NC_CACHE = {}


def kernel(**inputs):
    in_maps = prep_in_maps(**inputs)
    if "nc" not in _NC_CACHE:
        _NC_CACHE["nc"] = build_nc()
    nc = _NC_CACHE["nc"]
    res = run_bass_kernel_spmd(nc, in_maps, core_ids=list(range(8)))
    out = np.empty((BB, L, DM), np.float32)
    for c in range(8):
        b, h = c // 2, c % 2
        y = res.results[c]["y"]
        for el in range(4):
            e8 = 2 * el + h
            out[b, e8 * 512:(e8 + 1) * 512] = y[el * 512:(el + 1) * 512]
    return out


# version 17
# speedup vs baseline: 1.0186x; 1.0186x over previous
import numpy as np
from contextlib import ExitStack

import concourse.bass as bass
import concourse.mybir as mybir
from concourse.bass import IndirectOffsetOnAxis
from concourse.bass_utils import run_bass_kernel_spmd

F32 = mybir.dt.float32
BF16 = mybir.dt.bfloat16
I32 = mybir.dt.int32
AF = mybir.ActivationFunctionType
ALU = mybir.AluOpType
AX = mybir.AxisListType

THRESH = [0.2441942809699397, 0.46396913384288535, 0.6861859295255305,
          0.9059607823984762, 1.083978413225562]

BB, L, DM, FF = 4, 4096, 1024, 4096
LLOC = L // 2
DV = 512
DC = DM // 128
FC = FF // 128
NT = LLOC // 128
NCHUNK = L // 128
LN_EPS = 1e-5

CC1W = 1024
HBLK = 3328

ENGINES = ("tensor", "vector", "scalar", "sync", "gpsimd")


class Sem:
    def __init__(self, es, nc, name, eng=None):
        self.h = es.enter_context(nc.semaphore(name))
        self.n = 0
        self.eng = eng

    def at(self):
        return (self, self.n)


class Sched:

    def __init__(self, eng_sems=None):
        self.q = {e: [] for e in ENGINES}
        self.eng_sems = eng_sems or {}

    def add(self, eng, fn, waits=(), incs=(), dma=False):
        es = self.eng_sems.get(eng)
        waits = list(waits)
        incs = list(incs)
        if es is not None:
            waits.append(es.at())
            if not dma and not any(s is es for (s, k) in incs):
                incs.append((es, 1))
        w = [(s.h, v) for (s, v) in waits]
        il = []
        for (s, k) in incs:
            s.n += k
            il.append((s.h, k))
        self.q[eng].append((w, fn, il))

    def emit(self, block):
        for eng in ENGINES:
            items = self.q[eng]
            if not items:
                continue

            def body(e, items=items):
                for (waits, fn, incs) in items:
                    for (h, v) in waits:
                        if v > 0:
                            e.wait_ge(h, v)
                    instr = fn(e)
                    for (h, k) in incs:
                        assert instr is not None
                        instr = instr.then_inc(h, k)
            getattr(block, eng)(body)
        self.q = {e: [] for e in ENGINES}


def build_nc(upto="D"):
    nc = bass.Bass("TRN2", target_bir_lowering=False)

    x_in = nc.dram_tensor("x", [LLOC, DM], F32, kind="ExternalInput").ap()
    wfcT = nc.dram_tensor("wfcT", [DM, FF], F32, kind="ExternalInput").ap()
    wpjT = nc.dram_tensor("wpjT", [FF, DM], F32, kind="ExternalInput").ap()
    wqkv = nc.dram_tensor("wqkv", [DM, 3 * DM], F32, kind="ExternalInput").ap()
    wb_in = nc.dram_tensor("wb", [128, DC], F32, kind="ExternalInput").ap()
    gfull = nc.dram_tensor("gfull", [128, DM], F32, kind="ExternalInput").ap()
    bfull = nc.dram_tensor("bfull", [128, DM], F32, kind="ExternalInput").ap()
    ident_in = nc.dram_tensor("ident", [128, 128], F32, kind="ExternalInput").ap()
    twoid_in = nc.dram_tensor("twoid", [128, 128], F32, kind="ExternalInput").ap()
    masksu_in = nc.dram_tensor("masksu", [128, 128], F32, kind="ExternalInput").ap()
    maskiu_in = nc.dram_tensor("maskiu", [128, 128], F32, kind="ExternalInput").ap()
    idxst_in = nc.dram_tensor("idxst", [128, 20], I32, kind="ExternalInput").ap()
    idxbh_in = nc.dram_tensor("idxbh", [128, 4], I32, kind="ExternalInput").ap()
    idxo_in = nc.dram_tensor("idxo", [128, 32], I32, kind="ExternalInput").ap()
    y_out = nc.dram_tensor("y", [LLOC, DM], F32, kind="ExternalOutput").ap()

    x1_dram = nc.dram_tensor("x1_dram", [LLOC, DM], F32).ap()
    cc1_in = nc.dram_tensor("cc1_in", [2 * HBLK, CC1W], BF16)
    cc1_out = nc.dram_tensor("cc1_out", [2 * 8 * HBLK, CC1W], BF16, addr_space="Shared")
    cc1c_in = nc.dram_tensor("cc1c_in", [256, 16], F32)
    cc1c_out = nc.dram_tensor("cc1c_out", [2 * 8 * 128, 16], F32, addr_space="Shared")
    ccv_in = nc.dram_tensor("ccv_in", [2 * LLOC, DV], BF16)
    ccv_out = nc.dram_tensor("ccv_out", [2 * 8 * LLOC, DV], BF16, addr_space="Shared")
    cc2_in = nc.dram_tensor("cc2_in", [L, DV], BF16)
    cc2_out = nc.dram_tensor("cc2_out", [2 * 8 * LLOC, DV], BF16, addr_space="Shared")

    cc1v = cc1_in.ap()
    qfw = [cc1v[g * HBLK + 1024:g * HBLK + 2048, :].rearrange(
        "(p c) t -> p c t", p=128) for g in range(2)]
    kfw = [cc1v[g * HBLK + 2048:g * HBLK + 3072, :].rearrange(
        "(p c) t -> p c t", p=128) for g in range(2)]
    ktw = [cc1v[g * HBLK:g * HBLK + 1024, :].rearrange(
        "(p c) f -> p c f", p=128) for g in range(2)]
    xlw = [cc1v[g * HBLK + 3072:g * HBLK + 3328, :].rearrange(
        "(p c) w -> p c w", p=128) for g in range(2)]
    vw = [ccv_in.ap()[g * 2048:(g + 1) * 2048, :].rearrange(
        "(p d c) w -> p d c w", p=128, d=2) for g in range(2)]

    es = ExitStack()
    S_PE = Sem(es, nc, "s_pe")
    S_DVE = Sem(es, nc, "s_dve")
    S_ACT = Sem(es, nc, "s_act")
    S_GP = Sem(es, nc, "s_gp")

    DSEMS = []

    def dsem(name, eng):
        s = Sem(es, nc, "q_" + name, eng)
        DSEMS.append(s)
        return s

    CONS = dsem("cons", "sync")
    CONG = dsem("cong", "gpsimd")
    WFC = [dsem("wfc0", "gpsimd"), dsem("wfc1", "gpsimd")]
    WPJ = [dsem("wpj0", "gpsimd"), dsem("wpj1", "gpsimd")]
    WQKV = dsem("wqkv", "gpsimd")
    XLD = [dsem(f"xld{i}", "sync") for i in range(3)]
    X1O = dsem("x1o", "sync")
    DLD = [dsem("dld0", "gpsimd"), dsem("dld1", "gpsimd")]
    DXL = [dsem("dxl0", "sync"), dsem("dxl1", "sync")]
    QST = dsem("qst", "scalar")
    KFST = dsem("kfst", "scalar")
    KTST = [dsem("ktst0", "scalar"), dsem("ktst1", "scalar")]
    VTST = [dsem("vtst0", "scalar"), dsem("vtst1", "scalar")]
    CHT = [dsem("cht0", "gpsimd"), dsem("cht1", "gpsimd")]
    OST = [dsem("ost0", "scalar"), dsem("ost1", "scalar")]
    BRDY = dsem("brdy", "scalar")
    XLST = [dsem("xlst0", "scalar"), dsem("xlst1", "scalar")]
    CC = dsem("cc", "gpsimd")
    CCA = dsem("cca", "gpsimd")
    CCB = dsem("ccb", "gpsimd")
    SQG = [dsem("sqg0", "gpsimd"), dsem("sqg1", "gpsimd")]
    BH2S = [dsem(f"bh2s{i}", "gpsimd") for i in range(4)]

    id_b = es.enter_context(nc.sbuf_tensor("id_b", [128, 128], BF16))
    wb_sb = es.enter_context(nc.sbuf_tensor("wb_sb", [128, DC], BF16))
    bh2 = es.enter_context(nc.sbuf_tensor("bh2", [128, 64], F32))
    idxst = es.enter_context(nc.sbuf_tensor("idxst_s", [128, 20], I32))
    idxbh = es.enter_context(nc.sbuf_tensor("idxbh_s", [128, 4], I32))
    idxo = es.enter_context(nc.sbuf_tensor("idxo_s", [128, 32], I32))

    fb_guard = [0]


    def load_consts(sc):
        for (dst, src) in ((id_b, ident_in), (wb_sb, wb_in), (idxst, idxst_in),
                           (idxbh, idxbh_in), (idxo, idxo_in)):
            sc.add("gpsimd", lambda e, dst=dst, src=src: e.dma_start(
                out=dst[:], in_=src), incs=[(CONG, 16)], dma=True)

    def load_ffn_weights(sc, wfc_sb, wpj_sb):
        wfc_v = wfcT.rearrange("(c p) f -> p c f", p=128)
        wpj_v = wpjT.rearrange("(c p) f -> p c f", p=128)
        wfc3 = wfc_sb[:].rearrange("p (c f) -> p c f", c=DC)
        wpj3 = wpj_sb[:].rearrange("p (c f) -> p c f", c=FC)
        for half in range(2):
            fs = slice(half * (FF // 2), (half + 1) * (FF // 2))
            sc.add("gpsimd", lambda e, fs=fs: e.dma_start(
                out=wfc3[:, :, fs], in_=wfc_v[:, :, fs]),
                incs=[(WFC[half], 16)], dma=True)
        for half in range(2):
            cs = slice(half * (FC // 2), (half + 1) * (FC // 2))
            sc.add("gpsimd", lambda e, cs=cs: e.dma_start(
                out=wpj3[:, cs, :], in_=wpj_v[:, cs, :]),
                incs=[(WPJ[half], 16)], dma=True)

    def emit_ln(sc, x_ap, xn_ap, stats, slot, waits):
        st6 = stats["st6"]; st2 = stats["st2"]
        slot = slot % 4
        s6 = slice(slot * 12, slot * 12 + 12)
        mu = st2[:, 2 * slot:2 * slot + 1]
        var = st2[:, 2 * slot + 1:2 * slot + 2]
        rstd = stats["rstd"][:, slot:slot + 1]
        sc.add("vector", lambda e: e.bn_stats(
            st6[:, slot * 12:slot * 12 + 6], x_ap[:, 0:512]),
            waits=waits, incs=[(S_DVE, 1)])
        sc.add("vector", lambda e: e.bn_stats(
            st6[:, slot * 12 + 6:slot * 12 + 12], x_ap[:, 512:1024]),
            waits=waits, incs=[(S_DVE, 1)])
        sc.add("vector", lambda e: e.bn_aggr(st2[:, 2 * slot:2 * slot + 2],
                                             st6[:, s6]),
               waits=[S_DVE.at()], incs=[(S_DVE, 1)])
        sc.add("vector", lambda e: e.tensor_scalar_add(var, var, LN_EPS),
               waits=[S_DVE.at()], incs=[(S_DVE, 1)])
        sc.add("scalar", lambda e: e.activation(stats["lnt"][:, slot:slot + 1],
                                                var, AF.Ln),
               waits=[S_DVE.at()], incs=[(S_ACT, 1)])
        sc.add("scalar", lambda e: e.activation(rstd, stats["lnt"][:, slot:slot + 1],
                                                AF.Exp, scale=-0.5),
               waits=[S_ACT.at()], incs=[(S_ACT, 1)])
        sc.add("vector", lambda e: e.tensor_scalar(xn_ap, x_ap, mu, rstd,
                                                   ALU.subtract, ALU.mult),
               waits=[S_ACT.at()], incs=[(S_DVE, 1)])

    def emit_transpose_tile(sc, src_fn, dst_fm_ap, pst_ap, waits):
        def pe_ops(e):
            ins = None
            for c in range(DC):
                ins = e.transpose(pst_ap[:, c * 128:(c + 1) * 128], src_fn(c), id_b[:])
            return ins
        sc.add("tensor", pe_ops, waits=list(waits) + [(CONG, 80)],
               incs=[(S_PE, 1)])
        sc.add("vector", lambda e: e.tensor_copy(dst_fm_ap, pst_ap[:]),
               waits=[S_PE.at()], incs=[(S_DVE, 1)])

    def emit_stair_evac(sc, ps_y_list, bufs, waits):
        ybf = bufs[0]
        for ts, ps in enumerate(ps_y_list):
            sc.add("scalar", lambda e, ts=ts, ps=ps: e.activation(
                ybf[:, ts * DM:(ts + 1) * DM], ps, AF.Copy),
                waits=list(waits) + [(S_DVE, fb_guard[0])], incs=[(S_ACT, 1)])
        return S_ACT.at()

    def stair_dve_steps(sc, x_res_ap, out_ap, bufs, evac_at, out_waits):
        ybf, accA, accB = bufs

        def s0():
            sc.add("vector", lambda e: e.tensor_single_scalar(
                accA[:], ybf[:], THRESH[0], ALU.is_gt), waits=[evac_at])

        def mk(dst, src, k):
            def s():
                sc.add("vector", lambda e: e.scalar_tensor_tensor(
                    dst[:], ybf[:], THRESH[k], src[:], ALU.is_gt, ALU.add))
            return s

        def sf():
            sc.add("vector", lambda e: e.scalar_tensor_tensor(
                out_ap, accA[:], 0.1, x_res_ap, ALU.mult, ALU.add),
                waits=out_waits, incs=[(S_DVE, 1)])
            fb_guard[0] = S_DVE.n
        return [s0, mk(accB, accA, 1), mk(accA, accB, 2), mk(accB, accA, 3),
                mk(accA, accB, 4), sf]

    def phase_end(sc, exclude=()):
        for s in DSEMS:
            if s in exclude:
                continue
            sc.add(s.eng, lambda e: e.nop(), waits=[s.at()])

    def alloc_ffn_bufs(ph, wfc_sb, wpj_sb, px):
        fb = {"wfc": wfc_sb, "wpj": wpj_sb}
        fb["xst"] = [ph.enter_context(nc.sbuf_tensor(f"{px}xst{i}", [128, 2 * DM], F32))
                     for i in range(3)]
        x1o = ph.enter_context(nc.sbuf_tensor(f"{px}x1o", [128, 2 * DM], F32))
        fb["x1o"] = [x1o, x1o]
        fb["xnbf"] = [ph.enter_context(nc.sbuf_tensor(f"{px}xnbf{i}", [128, DM], BF16))
                      for i in range(2)]
        fb["xnfm"] = [ph.enter_context(nc.sbuf_tensor(f"{px}xnfm{i}", [128, DC * 256], BF16))
                      for i in range(2)]
        fb["hrelu"] = [ph.enter_context(nc.sbuf_tensor(f"{px}hrelu{i}", [128, 256], BF16))
                       for i in range(2)]
        fb["hsq"] = [ph.enter_context(nc.sbuf_tensor(f"{px}hsq{i}", [128, 256], BF16))
                     for i in range(2)]
        fb["stats"] = {
            "st6": ph.enter_context(nc.sbuf_tensor(px + "sst6", [128, 48], F32)),
            "st2": ph.enter_context(nc.sbuf_tensor(px + "sst2", [128, 16], F32)),
            "rstd": ph.enter_context(nc.sbuf_tensor(px + "srstd", [128, 8], F32)),
            "lnt": ph.enter_context(nc.sbuf_tensor(px + "slnt", [128, 8], F32)),
        }
        fb["stair"] = tuple(ph.enter_context(
            nc.sbuf_tensor(f"{px}stb{k}", [128, 2 * DM], BF16)) for k in range(3))
        fb["pst"] = [ph.enter_context(nc.psum_tensor(f"{px}pst{i}", [128, DM], BF16))
                     for i in range(2)]
        fb["ps_h"] = [ph.enter_context(nc.psum_tensor(f"{px}ps_h{i}", [128, 512], F32))
                      for i in range(2)]
        fb["ps_y"] = [ph.enter_context(nc.psum_tensor(f"{px}ps_y{i}", [128, DM], F32))
                      for i in range(2)]
        fb["pst_free"] = [0, 0]
        fb["xnbf_free"] = [0, 0]
        fb["h_act"] = [0, 0]
        fb["hsq_done"] = [0, 0]
        fb["y_free"] = (S_ACT, 0)
        fb["slot_free"] = [[], [], []]
        fb["x1o_free"] = 0
        return fb

    def emit_ffn_phase(sc, fb, x_src_dram, y_dst_dram, n_tiles, w_wfc, w_wpj,
                       pre_fns=None):
        wfc3 = fb["wfc"][:].rearrange("p (c f) -> p c f", c=DC)
        wpj3 = fb["wpj"][:].rearrange("p (c f) -> p c f", c=FC)
        xnfm2 = [fb["xnfm"][i][:].rearrange("p (c t) -> p c t", c=DC)
                 for i in range(2)]
        n_st = n_tiles // 2
        xn_ready = [[], [], []]
        st_ready = [None, None]
        ln_done = [None, None]

        def u_load(st, j):
            sl = st % 3
            if j == 0:
                xn_ready[sl] = []
            x_tile = fb["xst"][sl][:, j * DM:(j + 1) * DM]
            if pre_fns is None:
                tile = st * 2 + j
                sc.add("sync", lambda e, tile=tile, x_tile=x_tile: e.dma_start(
                    out=x_tile, in_=x_src_dram[tile * 128:(tile + 1) * 128, :]),
                    waits=list(fb["slot_free"][sl]), incs=[(XLD[sl], 16)])
            else:
                pre_fns[0](sc, st, j, fb["slot_free"][sl])

        def u_preln(st, j):
            sl = st % 3
            x_tile = fb["xst"][sl][:, j * DM:(j + 1) * DM]
            if pre_fns is None:
                st_ready[j] = [XLD[sl].at()]
            else:
                st_ready[j] = pre_fns[1](sc, st, j, x_tile)

        def u_ffnln(st, j):
            sl = st % 3
            x_tile = fb["xst"][sl][:, j * DM:(j + 1) * DM]
            emit_ln(sc, x_tile, fb["xnbf"][j][:], fb["stats"], (st * 2 + j) % 8,
                    st_ready[j] + [(S_PE, fb["xnbf_free"][j])])
            ln_done[j] = S_DVE.at()

        def u_trans(st, j):
            xnfm = xnfm2[st % 2]
            emit_transpose_tile(
                sc, lambda c, j=j: fb["xnbf"][j][:, c * 128:(c + 1) * 128],
                xnfm[:, :, j * 128:(j + 1) * 128], fb["pst"][j][:],
                [ln_done[j], (S_DVE, fb["pst_free"][j])])
            fb["pst_free"][j] = S_DVE.n
            fb["xnbf_free"][j] = S_PE.n
            xn_ready[st % 3].append(S_DVE.at())

        def stage1_micro(st):
            return [lambda: u_load(st, 0), lambda: u_load(st, 1),
                    lambda: u_preln(st, 0), lambda: u_preln(st, 1),
                    lambda: u_ffnln(st, 0), lambda: u_ffnln(st, 1),
                    lambda: u_trans(st, 0), lambda: u_trans(st, 1)]

        PLAN_F = [1, 3, 5, 11, 17, 22, 27, 29]

        def emit_mm1(f, xnfm, xr):
            hp = f % 2
            ps_h = fb["ps_h"][hp]
            hrelu = fb["hrelu"][hp]
            hsq = fb["hsq"][hp]

            def mm1(e, f=f, ps_h=ps_h, xnfm=xnfm):
                ins = None
                for d in range(DC):
                    ins = e.matmul(ps_h[:, 0:256],
                                   wfc3[:, d, f * 128:(f + 1) * 128],
                                   xnfm[:, d, :], start=(d == 0),
                                   stop=(d == DC - 1))
                return ins
            sc.add("tensor", mm1,
                   waits=list(xr) + [(WFC[f // 16], w_wfc[f // 16]),
                                     (S_ACT, fb["h_act"][hp])],
                   incs=[(S_PE, 1)])
            sc.add("scalar", lambda e, hrelu=hrelu, ps_h=ps_h: e.activation(
                hrelu[:], ps_h[:, 0:256], AF.Relu),
                waits=[S_PE.at()], incs=[(S_ACT, 1)])
            fb["h_act"][hp] = S_ACT.n
            sc.add("vector", lambda e, hsq=hsq, hrelu=hrelu: e.tensor_mul(
                hsq[:], hrelu[:], hrelu[:]),
                waits=[S_ACT.at()], incs=[(S_DVE, 1)])
            fb["hsq_done"][hp] = S_DVE.n

        def emit_mm2(f):
            hp = f % 2
            hsq = fb["hsq"][hp]
            w_mm2 = [(S_DVE, fb["hsq_done"][hp]),
                     (WPJ[f // 16], w_wpj[f // 16])]
            if f == 0:
                w_mm2.append(fb["y_free"])

            def mm2(e, f=f, hsq=hsq):
                ins = None
                for ts in range(2):
                    for dh in range(2):
                        ins = e.matmul(
                            fb["ps_y"][ts][:, dh * 512:(dh + 1) * 512],
                            hsq[:, ts * 128:(ts + 1) * 128],
                            wpj3[:, f, dh * 512:(dh + 1) * 512],
                            start=(f == 0), stop=(f == FC - 1))
                return ins
            sc.add("tensor", mm2, waits=w_mm2, incs=[(S_PE, 1)])

        def stage2(st, micro, plan_f):
            sl = st % 3
            hp2 = st % 2
            xst = fb["xst"][sl]
            x1o = fb["x1o"][hp2]
            xr = xn_ready[sl]
            mi = 0
            for f in range(FC):
                emit_mm1(f, xnfm2[st % 2], xr)
                if f >= 1:
                    emit_mm2(f - 1)
                while mi < len(micro) and f >= plan_f[mi]:
                    micro[mi]()
                    mi += 1
            while mi < len(micro):
                micro[mi]()
                mi += 1
            emit_mm2(FC - 1)
            mm2_done = S_PE.at()
            evac_at = emit_stair_evac(sc, [fb["ps_y"][0][:], fb["ps_y"][1][:]],
                                      fb["stair"], [mm2_done])
            fb["y_free"] = S_ACT.at()
            dve_steps = stair_dve_steps(
                sc, xst[:, 0:2 * DM], x1o[:, 0:2 * DM], fb["stair"], evac_at,
                [(X1O, fb["x1o_free"])])

            def ystore():
                sc.add("scalar", lambda e, st=st, x1o=x1o: e.dma_start(
                    out=y_dst_dram[st * 256:(st + 1) * 256, :].rearrange(
                        "(j p) d -> p j d", p=128),
                    in_=x1o[:, 0:2 * DM].rearrange("p (j d) -> p j d", j=2)),
                    waits=[S_DVE.at()], incs=[(X1O, 16)], dma=True)
                fb["slot_free"][sl] = [S_DVE.at(), S_ACT.at()]
                fb["x1o_free"] = X1O.n
            return dve_steps + [ystore]

        TAIL_F = [0, 0, 1, 2, 3, 4, 5]
        for s1 in stage1_micro(0):
            s1()
        tail = []
        for it in range(1, n_st + 1):
            micro = list(tail)
            plan = TAIL_F[:len(tail)]
            if it < n_st:
                micro += stage1_micro(it)
                plan = plan + [f + 6 for f in PLAN_F]
            tail = stage2(it - 1, micro, plan)
        for s in tail:
            s()

    with nc.Block() as blk:
        sc = Sched({'vector': S_DVE, 'scalar': S_ACT, 'gpsimd': S_GP})
        with ExitStack() as ph:
            wfc_sb = ph.enter_context(nc.sbuf_tensor("wfcA", [128, DC * FF], BF16))
            wpj_sb = ph.enter_context(nc.sbuf_tensor("wpjA", [128, FC * DM], BF16))
            load_consts(sc)
            load_ffn_weights(sc, wfc_sb, wpj_sb)
            fb = alloc_ffn_bufs(ph, wfc_sb, wpj_sb, "a")
            emit_ffn_phase(sc, fb, x_in, x1_dram, NT,
                           (WFC[0].n, WFC[1].n), (WPJ[0].n, WPJ[1].n))
            phase_end(sc)
            sc.emit(blk)

    def debug_dump(srcs):
        with nc.Block() as blk:
            sc = Sched({'vector': S_DVE, 'scalar': S_ACT, 'gpsimd': S_GP})
            with ExitStack() as ph:
                buf = ph.enter_context(nc.sbuf_tensor("dbg", [128, DM], F32))
                n = 0
                for (src, rows, w) in srcs:
                    for t in range(rows // 128):
                        if n >= NT:
                            break
                        sc.add("gpsimd", lambda e, src=src, t=t, w=w: e.dma_start(
                            out=buf[:, 0:w], in_=src[t * 128:(t + 1) * 128, 0:w]),
                            waits=[CONG.at()], incs=[(CONG, 16)], dma=True)
                        sc.add("gpsimd", lambda e, n=n, w=w: e.dma_start(
                            out=y_out[n * 128:(n + 1) * 128, 0:w], in_=buf[:, 0:w]),
                            waits=[CONG.at()], incs=[(CONG, 16)], dma=True)
                        n += 1
                phase_end(sc)
                sc.emit(blk)
        es.close()
        return nc

    if upto == "A":
        return debug_dump([(x1_dram, LLOC, DM)])

    with nc.Block() as blk:
        sc = Sched({'vector': S_DVE, 'scalar': S_ACT, 'gpsimd': S_GP})
        with ExitStack() as ph:
            wqkv_sb = ph.enter_context(
                nc.sbuf_tensor("wqkvB", [128, DC * 3 * DM], BF16))
            wq3 = wqkv_sb[:].rearrange("p (c f) -> p c f", c=DC)
            wq_v = wqkv.rearrange("(c p) f -> p c f", p=128)
            for half in range(2):
                fs = slice(half * (3 * DM // 2), (half + 1) * (3 * DM // 2))
                sc.add("gpsimd", lambda e, fs=fs: e.dma_start(
                    out=wq3[:, :, fs], in_=wq_v[:, :, fs]), incs=[(WQKV, 16)], dma=True)
            w_ld = WQKV.n

            bhat_all = ph.enter_context(nc.sbuf_tensor("bhat_all", [128, NT], F32))
            vscale_all = ph.enter_context(nc.sbuf_tensor("vscale_all", [128, NT], F32))
            ssm_all = ph.enter_context(nc.sbuf_tensor("ssm_all", [128, NT], F32))
            vs_tmp = ph.enter_context(nc.sbuf_tensor("vs_tmp", [128, NT], F32))
            id_f = ph.enter_context(nc.sbuf_tensor("id_f", [128, 128], F32))
            two_id = ph.enter_context(nc.sbuf_tensor("two_id", [128, 128], F32))
            masksu = ph.enter_context(nc.sbuf_tensor("masksu_s", [128, 128], F32))
            maskiu = ph.enter_context(nc.sbuf_tensor("maskiu_s", [128, 128], F32))
            for (dst, srcc) in ((id_f, ident_in), (two_id, twoid_in),
                                (masksu, masksu_in), (maskiu, maskiu_in)):
                sc.add("sync", lambda e, dst=dst, srcc=srcc: e.dma_start(
                    out=dst[:], in_=srcc), incs=[(CONS, 16)])
            xst = [ph.enter_context(nc.sbuf_tensor(f"bx{i}", [128, 2 * DM], F32))
                   for i in range(2)]
            xnbf = [ph.enter_context(nc.sbuf_tensor(f"bxn{i}", [128, DM], BF16))
                    for i in range(2)]
            xnfm_t = ph.enter_context(nc.sbuf_tensor("bxnfm", [128, DC * 256], BF16))
            xnfm = xnfm_t[:].rearrange("p (c t) -> p c t", c=DC)
            qfst_t = ph.enter_context(nc.sbuf_tensor("bqf", [128, DC * 256], BF16))
            qfst = qfst_t[:].rearrange("p (c t) -> p c t", c=DC)
            kfst_t = ph.enter_context(nc.sbuf_tensor("bkf", [128, DC * 256], BF16))
            kfst = kfst_t[:].rearrange("p (c t) -> p c t", c=DC)
            ktbf = [ph.enter_context(nc.sbuf_tensor(f"bkt{i}", [128, DM], BF16))
                    for i in range(2)]
            vtbf = [ph.enter_context(nc.sbuf_tensor(f"bvt{i}", [128, DM], BF16))
                    for i in range(2)]
            stats = {
                "st6": ph.enter_context(nc.sbuf_tensor("bst6", [128, 48], F32)),
                "st2": ph.enter_context(nc.sbuf_tensor("bst2", [128, 16], F32)),
                "rstd": ph.enter_context(nc.sbuf_tensor("brstd", [128, 8], F32)),
                "lnt": ph.enter_context(nc.sbuf_tensor("blnt", [128, 8], F32)),
                "junk": ph.enter_context(nc.sbuf_tensor("bst_junk", [128, DM], F32)),
            }
            kss = ph.enter_context(nc.sbuf_tensor("bkss", [128, 8], F32))
            betas = ph.enter_context(nc.sbuf_tensor("bbetas", [128, 8], F32))
            rss = ph.enter_context(nc.sbuf_tensor("brss", [128, 8], F32))
            e_f = ph.enter_context(nc.sbuf_tensor("be_f", [128, 128], F32))
            mt_b = ph.enter_context(nc.sbuf_tensor("bmt", [128, 128], BF16))
            tm_b = ph.enter_context(nc.sbuf_tensor("btm", [128, 128], BF16))
            x_nb = [ph.enter_context(nc.sbuf_tensor(f"bxn2_{i}", [128, 128], BF16))
                    for i in range(2)]
            tx_b = ph.enter_context(nc.sbuf_tensor("btx", [128, 128], BF16))
            d_b = ph.enter_context(nc.sbuf_tensor("bd", [128, 128], BF16))
            xl_b = ph.enter_context(nc.sbuf_tensor("bxl", [128, 256], BF16))
            pst = ph.enter_context(nc.psum_tensor("bpst", [128, DM], BF16))
            ps_n = ph.enter_context(nc.psum_tensor("bpsn", [128, 512], F32))
            ps_q = [ph.enter_context(nc.psum_tensor(f"bpsq{i}", [128, 512], F32))
                    for i in range(2)]
            ps_kv = [ph.enter_context(nc.psum_tensor(f"bpskv{i}", [128, DM], F32))
                     for i in range(2)]

            pst_free = 0
            xnbf_free = [0, 0]
            q_evac = [0, 0]
            kv_evac = [0, 0]
            kv_dve = [0, 0]
            beta_done = 0
            slot_free = [[], []]
            for st in range(NT // 2):
                sl = st % 2
                xn_ready = []
                for j in range(2):
                    tile = st * 2 + j
                    x_tile = xst[sl][:, j * DM:(j + 1) * DM]
                    sc.add("sync", lambda e, tile=tile, x_tile=x_tile: e.dma_start(
                        out=x_tile, in_=x1_dram[tile * 128:(tile + 1) * 128, :]),
                        waits=list(slot_free[sl]), incs=[(XLD[sl], 16)])
                for j in range(2):
                    tile = st * 2 + j
                    x_tile = xst[sl][:, j * DM:(j + 1) * DM]
                    emit_ln(sc, x_tile, xnbf[j][:], stats, tile % 8,
                            [XLD[sl].at(), (S_PE, xnbf_free[j])])
                    ln_done = S_DVE.at()
                    emit_transpose_tile(
                        sc, lambda c, j=j: xnbf[j][:, c * 128:(c + 1) * 128],
                        xnfm[:, :, j * 128:(j + 1) * 128], pst[:],
                        [ln_done, (S_DVE, pst_free)])
                    pst_free = S_DVE.n
                    xnbf_free[j] = S_PE.n
                    xn_ready.append(S_DVE.at())
                slot_free[sl] = [S_DVE.at(), S_ACT.at()]
                for dco in range(DC):
                    qp = dco % 2

                    def qmm(e, dco=dco, qp=qp):
                        ins = None
                        for d in range(DC):
                            ins = e.matmul(ps_q[qp][:, 0:256],
                                           wq3[:, d, dco * 128:(dco + 1) * 128],
                                           xnfm[:, d, :], start=(d == 0),
                                           stop=(d == DC - 1))
                        return ins
                    sc.add("tensor", qmm,
                           waits=list(xn_ready) + [(WQKV, w_ld),
                                                   (S_DVE, q_evac[qp]),
                                                   (S_ACT, beta_done)],
                           incs=[(S_PE, 1)])
                    sc.add("vector", lambda e, dco=dco, qp=qp: e.tensor_scalar_mul(
                        qfst[:, dco, :], ps_q[qp][:, 0:256], 1.0 / 32.0),
                        waits=[S_PE.at(), QST.at()], incs=[(S_DVE, 1)])
                    q_evac[qp] = S_DVE.n
                sc.add("scalar", lambda e, st=st: e.dma_start(
                    out=qfw[st // 4][:, :, (st % 4) * 256:(st % 4 + 1) * 256],
                    in_=qfst_t[:].rearrange("p (c t) -> p c t", c=DC)),
                    waits=[S_DVE.at()], incs=[(QST, 16)], dma=True)
                for j in range(2):
                    tile = st * 2 + j

                    def kmm(e, j=j):
                        ins = None
                        for dh in range(2):
                            for d in range(DC):
                                ins = e.matmul(
                                    ps_kv[j][:, dh * 512:(dh + 1) * 512],
                                    xnfm[:, d, j * 128:(j + 1) * 128],
                                    wq3[:, d, DM + dh * 512:DM + (dh + 1) * 512],
                                    start=(d == 0), stop=(d == DC - 1))
                        return ins
                    sc.add("tensor", kmm,
                           waits=list(xn_ready) + [(S_ACT, kv_evac[j]),
                                                   (S_DVE, kv_dve[j])],
                           incs=[(S_PE, 1)])
                    kmm_done = S_PE.at()
                    sc.add("vector", lambda e, j=j: e.tensor_copy(
                        ktbf[j][:], ps_kv[j][:]),
                        waits=[kmm_done, KTST[j].at()], incs=[(S_DVE, 1)])
                    kt_done = S_DVE.at()
                    kv_dve[j] = S_DVE.n
                    sc.add("scalar", lambda e, j=j, tile=tile: e.activation(
                        stats["junk"][:], ps_kv[j][:], AF.Square,
                        accum_out=kss[:, tile % 8:tile % 8 + 1]),
                        waits=[kmm_done, kt_done], incs=[(S_ACT, 1)])
                    kv_evac[j] = S_ACT.n
                    g = tile // 8
                    sc.add("scalar", lambda e, j=j, g=g, tile=tile: e.dma_start(
                        out=ktw[g][:, tile % 8, :], in_=ktbf[j][:]),
                        waits=[kt_done], incs=[(KTST[j], 16)], dma=True)
                    emit_transpose_tile(
                        sc, lambda c, j=j: ktbf[j][:, c * 128:(c + 1) * 128],
                        kfst[:, :, j * 128:(j + 1) * 128], pst[:],
                        [kt_done, (S_DVE, pst_free), KFST.at()])
                    pst_free = S_DVE.n
                sc.add("scalar", lambda e, st=st: e.dma_start(
                    out=kfw[st // 4][:, :, (st % 4) * 256:(st % 4 + 1) * 256],
                    in_=kfst_t[:].rearrange("p (c t) -> p c t", c=DC)),
                    waits=[S_DVE.at()], incs=[(KFST, 16)], dma=True)
                for j in range(2):
                    tile = st * 2 + j

                    def vmm(e, j=j):
                        ins = None
                        for dh in range(2):
                            for d in range(DC):
                                ins = e.matmul(
                                    ps_kv[j][:, dh * 512:(dh + 1) * 512],
                                    xnfm[:, d, j * 128:(j + 1) * 128],
                                    wq3[:, d, 2 * DM + dh * 512:2 * DM + (dh + 1) * 512],
                                    start=(d == 0), stop=(d == DC - 1))
                        return ins
                    sc.add("tensor", vmm,
                           waits=[(S_ACT, kv_evac[j]), (S_DVE, kv_dve[j])],
                           incs=[(S_PE, 1)])
                    sc.add("scalar", lambda e, j=j: e.activation(
                        vtbf[j][:], ps_kv[j][:], AF.Silu),
                        waits=[S_PE.at(), VTST[j].at()], incs=[(S_ACT, 1)])
                    kv_evac[j] = S_ACT.n
                    for dh in range(2):
                        sc.add("scalar", lambda e, j=j, g=tile // 8, jj=tile % 8,
                               dh=dh: e.dma_start(
                            out=vw[g][:, dh, jj, :],
                            in_=vtbf[j][:, dh * 512:(dh + 1) * 512]),
                            waits=[S_ACT.at()], incs=[(VTST[j], 16)], dma=True)
                for j in range(2):
                    tile = st * 2 + j
                    s8 = slice(tile % 8, tile % 8 + 1)
                    ci = slice(tile, tile + 1)

                    def bmm(e, j=j):
                        ins = None
                        for d in range(DC):
                            ins = e.matmul(ps_q[0][:, 256 + j:256 + j + 1],
                                           xnfm[:, d, j * 128:(j + 1) * 128],
                                           wb_sb[:, d:d + 1],
                                           start=(d == 0), stop=(d == DC - 1))
                        return ins
                    sc.add("tensor", bmm,
                           waits=[(S_DVE, q_evac[0]), (CONG, 80),
                                  (S_ACT, beta_done)],
                           incs=[(S_PE, 1)])
                    sc.add("scalar", lambda e, j=j, s8=s8: e.activation(
                        betas[:, s8], ps_q[0][:, 256 + j:256 + j + 1], AF.Sigmoid),
                        waits=[S_PE.at()], incs=[(S_ACT, 1)])
                    beta_done = S_ACT.n
                    sc.add("vector", lambda e, s8=s8, ci=ci: e.tensor_single_scalar(
                        ssm_all[:, ci], kss[:, s8], 1e-24, ALU.max),
                        waits=[(S_ACT, kv_evac[j])], incs=[(S_DVE, 1)])
                    sc.add("vector", lambda e, s8=s8, ci=ci: e.reciprocal(
                        rss[:, s8], ssm_all[:, ci]),
                        waits=[S_DVE.at()], incs=[(S_DVE, 1)])
                    sc.add("vector", lambda e, s8=s8, ci=ci: e.tensor_mul(
                        bhat_all[:, ci], betas[:, s8], rss[:, s8]),
                        waits=[S_DVE.at(), (S_ACT, beta_done)], incs=[(S_DVE, 1)])
                for j in range(2):
                    tile = st * 2 + j
                    bh = bhat_all[:, tile:tile + 1]

                    def gmm(e, j=j):
                        ins = None
                        for d in range(DC):
                            kfs = kfst[:, d, j * 128:(j + 1) * 128]
                            ins = e.matmul(ps_n[:, 0:128], kfs, kfs,
                                           start=(d == 0), stop=(d == DC - 1))
                        return ins
                    sc.add("tensor", gmm, waits=[(S_DVE, S_DVE.n), (CONS, 64)],
                           incs=[(S_PE, 1)])
                    sc.add("vector", lambda e, bh=bh: e.scalar_tensor_tensor(
                        e_f[:], ps_n[:, 0:128], bh, masksu[:], ALU.mult, ALU.mult),
                        waits=[S_PE.at()], incs=[(S_DVE, 1)])
                    sc.add("vector", lambda e: e.tensor_sub(x_nb[0][:], id_f[:],
                                                            e_f[:]))
                    sc.add("vector", lambda e: e.tensor_add(mt_b[:], id_f[:],
                                                            e_f[:]),
                           incs=[(S_DVE, 1)])
                    sc.add("tensor", lambda e: e.transpose(pst[:, 0:128], mt_b[:],
                                                           id_b[:]),
                           waits=[(S_DVE, S_DVE.n), (S_DVE, pst_free)],
                           incs=[(S_PE, 1)])
                    sc.add("vector", lambda e: e.tensor_copy(tm_b[:], pst[:, 0:128]),
                           waits=[S_PE.at()], incs=[(S_DVE, 1)])
                    pst_free = S_DVE.n

                    def lmm(e, j=j):
                        ins = None
                        for d in range(DC):
                            ins = e.matmul(ps_n[:, 128:256],
                                           kfst[:, d, j * 128:(j + 1) * 128],
                                           qfst[:, d, j * 128:(j + 1) * 128],
                                           start=(d == 0), stop=(d == DC - 1))
                        return ins
                    sc.add("tensor", lmm, waits=[(S_DVE, S_DVE.n)], incs=[(S_PE, 1)])
                    sc.add("vector", lambda e: e.tensor_mul(xl_b[:, 128:256],
                                                            ps_n[:, 128:256],
                                                            maskiu[:]),
                           waits=[S_PE.at(), XLST[0].at(), XLST[1].at()],
                           incs=[(S_DVE, 1)])
                    xc = 0
                    for it in range(6):
                        sc.add("tensor", lambda e, xc=xc: e.matmul(
                            ps_n[:, 256:384], tm_b[:], x_nb[xc][:],
                            start=True, stop=True),
                            waits=[(S_DVE, S_DVE.n)], incs=[(S_PE, 1)])
                        sc.add("vector", lambda e: e.tensor_sub(
                            d_b[:], two_id[:], ps_n[:, 256:384]),
                            waits=[S_PE.at()], incs=[(S_DVE, 1)])
                        sc.add("tensor", lambda e, xc=xc: e.transpose(
                            pst[:, 128:256], x_nb[xc][:], id_b[:]),
                            incs=[(S_PE, 1)])
                        sc.add("vector", lambda e: e.tensor_copy(
                            tx_b[:], pst[:, 128:256]),
                            waits=[S_PE.at()], incs=[(S_DVE, 1)])
                        sc.add("tensor", lambda e: e.matmul(
                            ps_n[:, 384:512], tx_b[:], d_b[:],
                            start=True, stop=True),
                            waits=[(S_DVE, S_DVE.n)], incs=[(S_PE, 1)])
                        dst = xl_b[:, 0:128] if it == 5 else x_nb[1 - xc][:]
                        sc.add("vector", lambda e, dst=dst: e.tensor_copy(
                            dst, ps_n[:, 384:512]),
                            waits=[S_PE.at()], incs=[(S_DVE, 1)])
                        xc = 1 - xc
                    sc.add("scalar", lambda e, g=tile // 8, jj=tile % 8:
                           e.dma_start(
                        out=xlw[g][:, jj // 4, (jj % 4) * 256:(jj % 4) * 256 + 256],
                        in_=xl_b[:]),
                        waits=[(S_DVE, S_DVE.n)], incs=[(XLST[j], 16)], dma=True)
                if st == 3 or st == 7:
                    g = st // 4
                    hs = slice(g * 8, g * 8 + 8)
                    sc.add("scalar", lambda e, hs=hs: e.activation(
                        vs_tmp[:, hs], ssm_all[:, hs], AF.Ln),
                        waits=[(S_DVE, S_DVE.n)], incs=[(S_ACT, 1)])
                    sc.add("scalar", lambda e, hs=hs: e.activation(
                        vscale_all[:, hs], vs_tmp[:, hs], AF.Exp, scale=0.5),
                        waits=[S_ACT.at()], incs=[(S_ACT, 1)])
                    sc.add("scalar", lambda e, g=g, hs=hs: e.dma_start(
                        out=cc1c_in.ap()[g * 128:(g + 1) * 128, 0:8],
                        in_=bhat_all[:, hs]),
                        waits=[(S_DVE, S_DVE.n)], incs=[(QST, 16)], dma=True)
                    sc.add("scalar", lambda e, g=g, hs=hs: e.dma_start(
                        out=cc1c_in.ap()[g * 128:(g + 1) * 128, 8:16],
                        in_=vscale_all[:, hs]),
                        waits=[S_ACT.at()], incs=[(QST, 16)], dma=True)
                    sc.add("scalar", lambda e: e.nop(),
                           waits=[QST.at(), KFST.at(), KTST[0].at(), KTST[1].at(),
                                  VTST[0].at(), VTST[1].at(), XLST[0].at(),
                                  XLST[1].at()],
                           incs=[(BRDY, 1)], dma=True)
                    sc.add("gpsimd", lambda e, g=g: e.collective_compute(
                        "AllGather", ALU.bypass, replica_groups=[list(range(8))],
                        ins=[cc1_in.ap()[g * HBLK:(g + 1) * HBLK, :].opt()],
                        outs=[cc1_out.ap()[g * 8 * HBLK:(g + 1) * 8 * HBLK, :].opt()]),
                        waits=[(BRDY, g + 1)], incs=[(CCA, 1)], dma=True)
                    sc.add("gpsimd", lambda e, g=g: e.collective_compute(
                        "AllGather", ALU.bypass, replica_groups=[list(range(8))],
                        ins=[ccv_in.ap()[g * 2048:(g + 1) * 2048, :].opt()],
                        outs=[ccv_out.ap()[g * 8 * 2048:(g + 1) * 8 * 2048, :].opt()]),
                        incs=[(CCB, 1)], dma=True)
                    sc.add("gpsimd", lambda e, g=g: e.collective_compute(
                        "AllGather", ALU.bypass, replica_groups=[list(range(8))],
                        ins=[cc1c_in.ap()[g * 128:(g + 1) * 128, :].opt()],
                        outs=[cc1c_out.ap()[g * 1024:(g + 1) * 1024, :].opt()]),
                        incs=[(CC, 1)], dma=True)
            phase_end(sc, exclude=(CC, CCA, CCB))
            sc.emit(blk)

    if upto == "B":
        return debug_dump([(cc1v, LLOC, 1024)])

    with nc.Block() as blk:
        sc = Sched({'vector': S_DVE, 'scalar': S_ACT, 'gpsimd': S_GP})
        with ExitStack() as ph:
            sbq = [ph.enter_context(nc.sbuf_tensor(f"csbq{i}", [128, DC * 1024], BF16))
                   for i in range(2)]
            sbk = [ph.enter_context(nc.sbuf_tensor(f"csbk{i}", [128, DC * 1024], BF16))
                   for i in range(2)]
            sbkt = [ph.enter_context(nc.sbuf_tensor(f"cskt{i}", [128, DC * 1024],
                                                     BF16)) for i in range(2)]
            sbxl = [ph.enter_context(nc.sbuf_tensor(f"csxl{i}", [128, 2048], BF16))
                    for i in range(2)]
            sbv = [ph.enter_context(nc.sbuf_tensor(f"csv{i}", [128, DC * DV], BF16))
                   for i in range(2)]
            r_b = ph.enter_context(nc.sbuf_tensor("cr", [128, DV], BF16))
            u_b = ph.enter_context(nc.sbuf_tensor("cu", [128, DV], BF16))
            o_sb = [ph.enter_context(nc.sbuf_tensor(f"co{i}", [128, DV], BF16))
                    for i in range(2)]
            S_f = ph.enter_context(nc.sbuf_tensor("cSf", [128, DC * DV], F32))
            S_bf = ph.enter_context(nc.sbuf_tensor("cSb", [128, DC * DV], BF16))
            Sf3 = S_f[:].rearrange("p (c f) -> p c f", c=DC)
            Sb3 = S_bf[:].rearrange("p (c f) -> p c f", c=DC)
            ps_r = ph.enter_context(nc.psum_tensor("cpsr", [128, DV], F32))
            ps_k2 = ph.enter_context(nc.psum_tensor("cpsk2", [128, DV], F32))
            ps_o = ph.enter_context(nc.psum_tensor("cpso", [128, DV], F32))

            def emit_bh2(blkhalf):
                sc.add("gpsimd", lambda e, blkhalf=blkhalf: e.indirect_dma_start(
                    bh2[:, blkhalf * 16:(blkhalf + 1) * 16], None, cc1c_out.ap(),
                    IndirectOffsetOnAxis(idxbh[:, blkhalf:blkhalf + 1], 0)),
                    waits=[(CC, 1 if blkhalf % 2 == 0 else 2), (CONG, 80)],
                    incs=[(BH2S[blkhalf], 16)], dma=True)

            emit_bh2(0)
            emit_bh2(2)

            def emit_stage_gathers(s, guard_val=None):
                sb = s % 2
                ccw = 1 if s < 2 else 2
                guard = [(S_PE, guard_val)] if guard_val is not None else []
                fams = [(sbq[sb], cc1_out, DC * 1024), (sbk[sb], cc1_out, DC * 1024),
                        (sbkt[sb], cc1_out, DC * 1024), (sbxl[sb], cc1_out, 2048),
                        (sbv[sb], ccv_out, DC * DV)]
                for fam, (dst, srct, w) in enumerate(fams):
                    icol = s * 5 + fam
                    sc.add("gpsimd", lambda e, dst=dst, srct=srct, w=w, icol=icol:
                           e.indirect_dma_start(
                               dst[:, 0:w], None, srct.ap(),
                               IndirectOffsetOnAxis(idxst[:, icol:icol + 1], 0)),
                           waits=[(CC, ccw), (CONG, 80)] + guard,
                           incs=[(SQG[sb], 16)], dma=True)

            emit_stage_gathers(0)
            emit_stage_gathers(1)
            s_cast_done = 0
            r_evac = 0
            o_evac = 0
            stage_guard = [None, None]
            for i in range(NCHUNK):
                if i == 12 or i == 16:
                    stage_guard[0 if i == 12 else 1] = S_PE.n
                cb = i % 2
                sb = (i // 4) % 2
                j = ((i // 8) % 2) * 4 + i % 4
                qf3 = sbq[sb][:].rearrange("p (c t) -> p c t", c=DC)
                kf3 = sbk[sb][:].rearrange("p (c t) -> p c t", c=DC)
                sqw = (SQG[sb], 80 * (i // 16 + 1))
                xoff = (j // 4) * 1024 + (j % 4) * 256
                if i == 13:
                    emit_stage_gathers(2, stage_guard[0])
                if i == 15:
                    emit_bh2(1)
                    emit_bh2(3)
                if i == 17:
                    emit_stage_gathers(3, stage_guard[1])
                ld_w = [sqw]
                bhcol = ((i // 4) % 2) * 2 + (i // 16)
                bh = bh2[:, bhcol * 16 + j:bhcol * 16 + j + 1]
                vs = bh2[:, bhcol * 16 + 8 + j:bhcol * 16 + 8 + j + 1]

                bhw = (BH2S[bhcol], 16)
                if i > 0:
                    def ks0(e, kf3=kf3, j=j):
                        ins = None
                        for d in range(DC):
                            ins = e.matmul(ps_r[:], kf3[:, d, j * 128:(j + 1) * 128],
                                           Sb3[:, d, :],
                                           start=(d == 0), stop=(d == DC - 1))
                        return ins
                    sc.add("tensor", ks0, waits=ld_w + [(S_ACT, s_cast_done),
                                                        (S_DVE, r_evac)],
                           incs=[(S_PE, 1)])
                    sc.add("vector", lambda e, sb=sb, j=j, vs=vs:
                           e.scalar_tensor_tensor(
                        r_b[:], sbv[sb][:, j * DV:(j + 1) * DV], vs, ps_r[:],
                        ALU.mult, ALU.subtract),
                        waits=[S_PE.at(), (CONG, 80), bhw], incs=[(S_DVE, 1)])
                else:
                    sc.add("vector", lambda e, sb=sb, j=j, vs=vs: e.tensor_scalar(
                        r_b[:], sbv[sb][:, j * DV:(j + 1) * DV], vs, None,
                        ALU.mult),
                        waits=ld_w + [(CONG, 80), bhw], incs=[(S_DVE, 1)])
                r_evac = S_DVE.n
                if i > 0:
                    def qs0(e, qf3=qf3, j=j):
                        ins = None
                        for d in range(DC):
                            ins = e.matmul(ps_o[:], qf3[:, d, j * 128:(j + 1) * 128],
                                           Sb3[:, d, :],
                                           start=(d == 0), stop=False)
                        return ins
                    sc.add("tensor", qs0, waits=ld_w + [(S_ACT, s_cast_done),
                                                        (S_DVE, o_evac)],
                           incs=[(S_PE, 1)])
                sc.add("tensor", lambda e, sb=sb, xoff=xoff: e.matmul(
                    ps_r[:], sbxl[sb][:, xoff:xoff + 128], r_b[:],
                    start=True, stop=True),
                    waits=[(S_DVE, r_evac)], incs=[(S_PE, 1)])
                sc.add("vector", lambda e, bh=bh: e.tensor_scalar(
                    u_b[:], ps_r[:], bh, None, ALU.mult),
                    waits=[S_PE.at()], incs=[(S_DVE, 1)])
                u_done = S_DVE.at()
                def lumm(e, sb=sb, xoff=xoff, first=(i == 0)):
                    return e.matmul(ps_o[:], sbxl[sb][:, xoff + 128:xoff + 256],
                                    u_b[:], start=first, stop=True)
                lu_w = [u_done]
                if i == 0:
                    lu_w.append((S_DVE, o_evac))
                sc.add("tensor", lumm, waits=lu_w, incs=[(S_PE, 1)])
                sc.add("vector", lambda e, cb=cb: e.tensor_copy(o_sb[cb][:], ps_o[:]),
                       waits=[S_PE.at(), OST[cb].at()], incs=[(S_DVE, 1)])
                o_evac = S_DVE.n
                sc.add("scalar", lambda e, i=i, cb=cb: e.dma_start(
                    out=cc2_in.ap()[i * 128:(i + 1) * 128, :], in_=o_sb[cb][:]),
                    waits=[S_DVE.at()], incs=[(OST[cb], 16)], dma=True)
                def kumm_fn(d, ps):
                    def kumm(e, d=d, sb=sb, j=j, ps=ps):
                        return e.matmul(
                            ps[:], sbkt[sb][:, j * 1024 + d * 128:
                                            j * 1024 + (d + 1) * 128],
                            u_b[:], start=True, stop=True)
                    return kumm

                def s_add(d, ps, pe_at):
                    if i == 0:
                        sc.add("vector", lambda e, d=d, ps=ps: e.tensor_copy(
                            Sf3[:, d, :], ps[:]),
                            waits=[pe_at], incs=[(S_DVE, 1)])
                    else:
                        sc.add("vector", lambda e, d=d, ps=ps: e.tensor_add(
                            Sf3[:, d, :], Sf3[:, d, :], ps[:]),
                            waits=[pe_at], incs=[(S_DVE, 1)])
                    sc.add("scalar", lambda e, d=d: e.activation(
                        Sb3[:, d, :], Sf3[:, d, :], AF.Copy),
                        waits=[S_DVE.at()], incs=[(S_ACT, 1)])

                kps = [ps_r, ps_k2]
                pend = []
                for d in range(DC):
                    ps = kps[d % 2]
                    sc.add("tensor", kumm_fn(d, ps),
                           waits=[(S_DVE, S_DVE.n)], incs=[(S_PE, 1)])
                    pend.append((d, ps, S_PE.at()))
                    if len(pend) == 2:
                        dd, pp, pa = pend.pop(0)
                        s_add(dd, pp, pa)
                for (dd, pp, pa) in pend:
                    s_add(dd, pp, pa)
                r_evac = S_DVE.n
                s_cast_done = S_ACT.n

                if i % 8 == 7:
                    pp = i // 8
                    sc.add("scalar", lambda e: e.nop(),
                           waits=[OST[0].at(), OST[1].at()],
                           incs=[(BRDY, 1)], dma=True)
                    sc.add("gpsimd", lambda e, pp=pp: e.collective_compute(
                        "AllGather", ALU.bypass, replica_groups=[list(range(8))],
                        ins=[cc2_in.ap()[pp * 1024:(pp + 1) * 1024, :].opt()],
                        outs=[cc2_out.ap()[pp * 8 * 1024:(pp + 1) * 8 * 1024,
                                           :].opt()]),
                        waits=[(BRDY, 3 + pp)], incs=[(CC, 1)], dma=True)
            phase_end(sc, exclude=(CC, CCA, CCB))
            sc.emit(blk)

    if upto == "C":
        return debug_dump([(cc2_in.ap(), L, DV)])

    with nc.Block() as blk:
        sc = Sched({'vector': S_DVE, 'scalar': S_ACT, 'gpsimd': S_GP})
        with ExitStack() as ph:
            wfc_sb = ph.enter_context(nc.sbuf_tensor("wfcD", [128, DC * FF], BF16))
            wpj_sb = ph.enter_context(nc.sbuf_tensor("wpjD", [128, FC * DM], BF16))
            load_ffn_weights(sc, wfc_sb, wpj_sb)
            fb = alloc_ffn_bufs(ph, wfc_sb, wpj_sb, "d")
            g_sb = ph.enter_context(nc.sbuf_tensor("g_sb", [128, DM], F32))
            b_sb = ph.enter_context(nc.sbuf_tensor("b_sb", [128, DM], F32))
            for (dst, srcc) in ((g_sb, gfull), (b_sb, bfull)):
                sc.add("sync", lambda e, dst=dst, srcc=srcc: e.dma_start(
                    out=dst[:], in_=srcc), incs=[(CONS, 16)])
            x1t = [ph.enter_context(nc.sbuf_tensor(f"dx1_{i}", [128, DM], F32))
                   for i in range(2)]
            obf = [ph.enter_context(nc.sbuf_tensor(f"dob_{i}", [128, DM], BF16))
                   for i in range(2)]
            d_used = [[], []]

            def pre_load(sc, st, j, slot_free):
                tile = st * 2 + j
                guards = list(d_used[j]) + list(slot_free)
                sc.add("sync", lambda e, tile=tile, j=j: e.dma_start(
                    out=x1t[j][:], in_=x1_dram[tile * 128:(tile + 1) * 128, :]),
                    waits=guards, incs=[(DXL[j], 16)])
                for half in range(2):
                    icol = half * 16 + tile
                    sc.add("gpsimd", lambda e, j=j, half=half, icol=icol:
                           e.indirect_dma_start(
                               obf[j][:, half * DV:(half + 1) * DV], None,
                               cc2_out.ap(),
                               IndirectOffsetOnAxis(idxo[:, icol:icol + 1], 0)),
                           waits=guards + [(CC, 3 + tile // 4),
                                           (CONG, CONG.n)],
                           incs=[(DLD[j], 16)], dma=True)

            def pre_ln(sc, st, j, x_tile):
                tile = st * 2 + j
                ld = [DXL[j].at(), DLD[j].at()]
                emit_ln(sc, obf[j][:], x_tile, fb["stats"], tile % 8, ld)
                xn_done = S_DVE.at()
                sc.add("gpsimd", lambda e, x_tile=x_tile: e.tensor_mul(
                    x_tile, x_tile, g_sb[:]),
                    waits=[xn_done, (CONS, 96)])
                sc.add("gpsimd", lambda e, x_tile=x_tile: e.tensor_add(
                    x_tile, x_tile, b_sb[:]))
                sc.add("gpsimd", lambda e, j=j, x_tile=x_tile: e.tensor_add(
                    x_tile, x_tile, x1t[j][:]),
                    waits=[DXL[j].at()], incs=[(S_GP, 1)])
                d_used[j] = [xn_done, S_GP.at(), S_ACT.at()]
                return [S_GP.at()]

            emit_ffn_phase(sc, fb, None, y_out, NT,
                           (WFC[0].n, WFC[1].n), (WPJ[0].n, WPJ[1].n),
                           pre_fns=(pre_load, pre_ln))
            phase_end(sc)
            sc.emit(blk)

    es.close()
    return nc


def prep_in_maps(L0_attn_x, L0_attn_Wq, L0_attn_Wk, L0_attn_Wv, L0_attn_Wb,
                 L0_attn_ln_g, L0_attn_ln_b, L0_ffn_W_fc, L0_ffn_W_proj):
    x = np.ascontiguousarray(np.asarray(L0_attn_x, dtype=np.float32))
    wfcT_h = np.ascontiguousarray(np.asarray(L0_ffn_W_fc, np.float32).T)
    wpjT_h = np.ascontiguousarray(np.asarray(L0_ffn_W_proj, np.float32).T)
    Wq = np.asarray(L0_attn_Wq, np.float32)
    Wk = np.asarray(L0_attn_Wk, np.float32)
    Wv = np.asarray(L0_attn_Wv, np.float32)
    wb_h = np.ascontiguousarray(
        np.asarray(L0_attn_Wb, np.float32).reshape(DC, 128).T)
    gfull_h = np.ascontiguousarray(np.broadcast_to(
        np.asarray(L0_attn_ln_g, np.float32), (128, DM)))
    bfull_h = np.ascontiguousarray(np.broadcast_to(
        np.asarray(L0_attn_ln_b, np.float32), (128, DM)))
    ident_h = np.eye(128, dtype=np.float32)
    twoid_h = 2.0 * np.eye(128, dtype=np.float32)
    masksu_h = np.triu(np.ones((128, 128), np.float32), 1)
    maskiu_h = np.triu(np.ones((128, 128), np.float32), 0)

    p = np.arange(128, dtype=np.int64)
    in_maps = []
    for c in range(8):
        b, h = c // 2, c % 2
        wqkv_h = np.ascontiguousarray(np.concatenate([Wq, Wk, Wv], axis=1))
        HB, G8 = 3328, 8 * 3328
        idxst = np.zeros((128, 20), np.int64)
        for s in range(4):
            rank, g = 2 * b + s % 2, s // 2
            base = g * G8 + rank * HB
            idxst[:, s * 5 + 0] = base + 1024 + p * 8
            idxst[:, s * 5 + 1] = base + 2048 + p * 8
            idxst[:, s * 5 + 2] = base + p * 8
            idxst[:, s * 5 + 3] = base + 3072 + p * 2
            idxst[:, s * 5 + 4] = (g * 8 * 2048 + rank * 2048
                                   + p * 16 + h * 8)
        idxbh = np.zeros((128, 4), np.int64)
        for blk in range(2):
            for g in range(2):
                idxbh[:, blk * 2 + g] = g * 1024 + (2 * b + blk) * 128 + p
        idxo = np.zeros((128, 32), np.int64)
        for q in range(2):
            for t in range(NT):
                idxo[:, q * 16 + t] = ((t // 4) * 8 * 1024 + (2 * b + q) * 1024
                                       + h * 512 + (t % 4) * 128 + p)
        in_maps.append({
            "x": np.ascontiguousarray(np.concatenate(
                [x[b, (2 * el + h) * 512:(2 * el + h + 1) * 512]
                 for el in range(4)], axis=0)),
            "wfcT": wfcT_h, "wpjT": wpjT_h, "wqkv": wqkv_h, "wb": wb_h,
            "gfull": gfull_h, "bfull": bfull_h, "ident": ident_h,
            "twoid": twoid_h, "masksu": masksu_h, "maskiu": maskiu_h,
            "idxst": idxst.astype(np.int32),
            "idxbh": idxbh.astype(np.int32), "idxo": idxo.astype(np.int32),
        })
    return in_maps


_NC_CACHE = {}


def kernel(**inputs):
    in_maps = prep_in_maps(**inputs)
    if "nc" not in _NC_CACHE:
        _NC_CACHE["nc"] = build_nc()
    nc = _NC_CACHE["nc"]
    res = run_bass_kernel_spmd(nc, in_maps, core_ids=list(range(8)))
    out = np.empty((BB, L, DM), np.float32)
    for c in range(8):
        b, h = c // 2, c % 2
        y = res.results[c]["y"]
        for el in range(4):
            e8 = 2 * el + h
            out[b, e8 * 512:(e8 + 1) * 512] = y[el * 512:(el + 1) * 512]
    return out


# version 18 (final)
# speedup vs baseline: 1.0667x; 1.0472x over previous
import numpy as np
from contextlib import ExitStack

import concourse.bass as bass
import concourse.mybir as mybir
from concourse.bass import IndirectOffsetOnAxis
from concourse.bass_utils import run_bass_kernel_spmd

F32 = mybir.dt.float32
BF16 = mybir.dt.bfloat16
I32 = mybir.dt.int32
AF = mybir.ActivationFunctionType
ALU = mybir.AluOpType
AX = mybir.AxisListType

THRESH = [0.2441942809699397, 0.46396913384288535, 0.6861859295255305,
          0.9059607823984762, 1.083978413225562]

BB, L, DM, FF = 4, 4096, 1024, 4096
LLOC = L // 2
DV = 512
DC = DM // 128
FC = FF // 128
NT = LLOC // 128
NCHUNK = L // 128
LN_EPS = 1e-5

CC1W = 1024
HBLK = 3328

ENGINES = ("tensor", "vector", "scalar", "sync", "gpsimd")


class Sem:
    def __init__(self, es, nc, name, eng=None):
        self.h = es.enter_context(nc.semaphore(name))
        self.n = 0
        self.eng = eng

    def at(self):
        return (self, self.n)


class Sched:

    def __init__(self, eng_sems=None):
        self.q = {e: [] for e in ENGINES}
        self.eng_sems = eng_sems or {}

    def add(self, eng, fn, waits=(), incs=(), dma=False):
        es = self.eng_sems.get(eng)
        waits = list(waits)
        incs = list(incs)
        if es is not None:
            waits.append(es.at())
            if not dma and not any(s is es for (s, k) in incs):
                incs.append((es, 1))
        w = [(s.h, v) for (s, v) in waits]
        il = []
        for (s, k) in incs:
            s.n += k
            il.append((s.h, k))
        self.q[eng].append((w, fn, il))

    def emit(self, block):
        for eng in ENGINES:
            items = self.q[eng]
            if not items:
                continue

            def body(e, items=items):
                for (waits, fn, incs) in items:
                    for (h, v) in waits:
                        if v > 0:
                            e.wait_ge(h, v)
                    instr = fn(e)
                    for (h, k) in incs:
                        assert instr is not None
                        instr = instr.then_inc(h, k)
            getattr(block, eng)(body)
        self.q = {e: [] for e in ENGINES}


def build_nc(upto="D"):
    nc = bass.Bass("TRN2", target_bir_lowering=False)

    x_in = nc.dram_tensor("x", [LLOC, DM], F32, kind="ExternalInput").ap()
    wfcT = nc.dram_tensor("wfcT", [DM, FF], F32, kind="ExternalInput").ap()
    wpjT = nc.dram_tensor("wpjT", [FF, DM], F32, kind="ExternalInput").ap()
    wqkv = nc.dram_tensor("wqkv", [DM, 3 * DM], F32, kind="ExternalInput").ap()
    wb_in = nc.dram_tensor("wb", [128, DC], F32, kind="ExternalInput").ap()
    gfull = nc.dram_tensor("gfull", [128, DM], F32, kind="ExternalInput").ap()
    bfull = nc.dram_tensor("bfull", [128, DM], F32, kind="ExternalInput").ap()
    ident_in = nc.dram_tensor("ident", [128, 128], F32, kind="ExternalInput").ap()
    twoid_in = nc.dram_tensor("twoid", [128, 128], F32, kind="ExternalInput").ap()
    masksu_in = nc.dram_tensor("masksu", [128, 128], F32, kind="ExternalInput").ap()
    maskiu_in = nc.dram_tensor("maskiu", [128, 128], F32, kind="ExternalInput").ap()
    idxst_in = nc.dram_tensor("idxst", [128, 20], I32, kind="ExternalInput").ap()
    idxbh_in = nc.dram_tensor("idxbh", [128, 4], I32, kind="ExternalInput").ap()
    idxo_in = nc.dram_tensor("idxo", [128, 32], I32, kind="ExternalInput").ap()
    y_out = nc.dram_tensor("y", [LLOC, DM], F32, kind="ExternalOutput").ap()

    x1_dram = nc.dram_tensor("x1_dram", [LLOC, DM], F32).ap()
    cc1_in = nc.dram_tensor("cc1_in", [2 * HBLK, CC1W], BF16)
    cc1_out = nc.dram_tensor("cc1_out", [2 * 8 * HBLK, CC1W], BF16, addr_space="Shared")
    cc1c_in = nc.dram_tensor("cc1c_in", [256, 16], F32)
    cc1c_out = nc.dram_tensor("cc1c_out", [2 * 8 * 128, 16], F32, addr_space="Shared")
    ccv_in = nc.dram_tensor("ccv_in", [2 * LLOC, DV], BF16)
    ccv_out = nc.dram_tensor("ccv_out", [2 * 8 * LLOC, DV], BF16, addr_space="Shared")
    cc2_in = nc.dram_tensor("cc2_in", [L, DV], BF16)
    cc2_out = nc.dram_tensor("cc2_out", [2 * 8 * LLOC, DV], BF16, addr_space="Shared")

    cc1v = cc1_in.ap()
    qfw = [cc1v[g * HBLK + 1024:g * HBLK + 2048, :].rearrange(
        "(p c) t -> p c t", p=128) for g in range(2)]
    kfw = [cc1v[g * HBLK + 2048:g * HBLK + 3072, :].rearrange(
        "(p c) t -> p c t", p=128) for g in range(2)]
    ktw = [cc1v[g * HBLK:g * HBLK + 1024, :].rearrange(
        "(p c) f -> p c f", p=128) for g in range(2)]
    xlw = [cc1v[g * HBLK + 3072:g * HBLK + 3328, :].rearrange(
        "(p c) w -> p c w", p=128) for g in range(2)]
    vw = [ccv_in.ap()[g * 2048:(g + 1) * 2048, :].rearrange(
        "(p d c) w -> p d c w", p=128, d=2) for g in range(2)]

    es = ExitStack()
    S_PE = Sem(es, nc, "s_pe")
    S_DVE = Sem(es, nc, "s_dve")
    S_ACT = Sem(es, nc, "s_act")
    S_GP = Sem(es, nc, "s_gp")

    DSEMS = []

    def dsem(name, eng):
        s = Sem(es, nc, "q_" + name, eng)
        DSEMS.append(s)
        return s

    CONS = dsem("cons", "sync")
    CONG = dsem("cong", "gpsimd")
    WFC = [dsem("wfc0", "gpsimd"), dsem("wfc1", "gpsimd")]
    WPJ = [dsem("wpj0", "gpsimd"), dsem("wpj1", "gpsimd")]
    WQKV = [dsem(f"wqkv{i}", "gpsimd") for i in range(3)]
    XLD = [dsem(f"xld{i}", "sync") for i in range(3)]
    X1O = dsem("x1o", "sync")
    DLD = [dsem("dld0", "gpsimd"), dsem("dld1", "gpsimd")]
    DXL = [dsem("dxl0", "sync"), dsem("dxl1", "sync")]
    QST = dsem("qst", "scalar")
    KFST = dsem("kfst", "scalar")
    KTST = [dsem("ktst0", "scalar"), dsem("ktst1", "scalar")]
    VTST = [dsem("vtst0", "scalar"), dsem("vtst1", "scalar")]
    CHT = [dsem("cht0", "gpsimd"), dsem("cht1", "gpsimd")]
    OST = [dsem("ost0", "scalar"), dsem("ost1", "scalar")]
    BRDY = dsem("brdy", "scalar")
    XLST = [dsem("xlst0", "scalar"), dsem("xlst1", "scalar")]
    CC = dsem("cc", "gpsimd")
    CCA = dsem("cca", "gpsimd")
    CCB = dsem("ccb", "gpsimd")
    SQG = [dsem("sqg0", "gpsimd"), dsem("sqg1", "gpsimd")]
    BH2S = [dsem(f"bh2s{i}", "gpsimd") for i in range(4)]

    id_b = es.enter_context(nc.sbuf_tensor("id_b", [128, 128], BF16))
    wb_sb = es.enter_context(nc.sbuf_tensor("wb_sb", [128, DC], BF16))
    bh2 = es.enter_context(nc.sbuf_tensor("bh2", [128, 64], F32))
    idxst = es.enter_context(nc.sbuf_tensor("idxst_s", [128, 20], I32))
    wfc_sh = es.enter_context(nc.sbuf_tensor("wfcSH", [128, DC * (FF // 2)],
                                             BF16))
    idxbh = es.enter_context(nc.sbuf_tensor("idxbh_s", [128, 4], I32))
    idxo = es.enter_context(nc.sbuf_tensor("idxo_s", [128, 32], I32))

    fb_guard = [0]


    def load_consts(sc):
        for (dst, src) in ((id_b, ident_in), (wb_sb, wb_in), (idxst, idxst_in),
                           (idxbh, idxbh_in), (idxo, idxo_in)):
            sc.add("gpsimd", lambda e, dst=dst, src=src: e.dma_start(
                out=dst[:], in_=src), incs=[(CONG, 16)], dma=True)

    def load_wfc_sh(sc):
        wfc_v = wfcT.rearrange("(c p) f -> p c f", p=128)
        wfc3 = wfc_sh[:].rearrange("p (c f) -> p c f", c=DC)
        sc.add("gpsimd", lambda e: e.dma_start(
            out=wfc3[:, :, :], in_=wfc_v[:, :, 0:FF // 2]),
            incs=[(WFC[0], 16)], dma=True)

    def load_wfc1(sc, wfc1_sb):
        wfc_v = wfcT.rearrange("(c p) f -> p c f", p=128)
        wfc3 = wfc1_sb[:].rearrange("p (c f) -> p c f", c=DC)
        sc.add("gpsimd", lambda e: e.dma_start(
            out=wfc3[:, :, :], in_=wfc_v[:, :, FF // 2:FF]),
            incs=[(WFC[1], 16)], dma=True)

    def load_wpj(sc, wpj_sb):
        wpj_v = wpjT.rearrange("(c p) f -> p c f", p=128)
        wpj3 = wpj_sb[:].rearrange("p (c f) -> p c f", c=FC)
        for half in range(2):
            cs = slice(half * (FC // 2), (half + 1) * (FC // 2))
            sc.add("gpsimd", lambda e, cs=cs: e.dma_start(
                out=wpj3[:, cs, :], in_=wpj_v[:, cs, :]),
                incs=[(WPJ[half], 16)], dma=True)

    def emit_ln(sc, x_ap, xn_ap, stats, slot, waits):
        st6 = stats["st6"]; st2 = stats["st2"]
        slot = slot % 4
        s6 = slice(slot * 12, slot * 12 + 12)
        mu = st2[:, 2 * slot:2 * slot + 1]
        var = st2[:, 2 * slot + 1:2 * slot + 2]
        rstd = stats["rstd"][:, slot:slot + 1]
        sc.add("vector", lambda e: e.bn_stats(
            st6[:, slot * 12:slot * 12 + 6], x_ap[:, 0:512]),
            waits=waits, incs=[(S_DVE, 1)])
        sc.add("vector", lambda e: e.bn_stats(
            st6[:, slot * 12 + 6:slot * 12 + 12], x_ap[:, 512:1024]),
            waits=waits, incs=[(S_DVE, 1)])
        sc.add("vector", lambda e: e.bn_aggr(st2[:, 2 * slot:2 * slot + 2],
                                             st6[:, s6]),
               waits=[S_DVE.at()], incs=[(S_DVE, 1)])
        sc.add("vector", lambda e: e.tensor_scalar_add(var, var, LN_EPS),
               waits=[S_DVE.at()], incs=[(S_DVE, 1)])
        sc.add("scalar", lambda e: e.activation(stats["lnt"][:, slot:slot + 1],
                                                var, AF.Ln),
               waits=[S_DVE.at()], incs=[(S_ACT, 1)])
        sc.add("scalar", lambda e: e.activation(rstd, stats["lnt"][:, slot:slot + 1],
                                                AF.Exp, scale=-0.5),
               waits=[S_ACT.at()], incs=[(S_ACT, 1)])
        sc.add("vector", lambda e: e.tensor_scalar(xn_ap, x_ap, mu, rstd,
                                                   ALU.subtract, ALU.mult),
               waits=[S_ACT.at()], incs=[(S_DVE, 1)])

    def emit_transpose_tile(sc, src_fn, dst_fm_ap, pst_ap, waits):
        def pe_ops(e):
            ins = None
            for c in range(DC):
                ins = e.transpose(pst_ap[:, c * 128:(c + 1) * 128], src_fn(c), id_b[:])
            return ins
        sc.add("tensor", pe_ops, waits=list(waits) + [(CONG, 80)],
               incs=[(S_PE, 1)])
        sc.add("vector", lambda e: e.tensor_copy(dst_fm_ap, pst_ap[:]),
               waits=[S_PE.at()], incs=[(S_DVE, 1)])

    def emit_stair_evac(sc, ps_y_list, bufs, waits):
        ybf = bufs[0]
        for ts, ps in enumerate(ps_y_list):
            sc.add("scalar", lambda e, ts=ts, ps=ps: e.activation(
                ybf[:, ts * DM:(ts + 1) * DM], ps, AF.Copy),
                waits=list(waits) + [(S_DVE, fb_guard[0])], incs=[(S_ACT, 1)])
        return S_ACT.at()

    def stair_dve_steps(sc, x_res_ap, out_ap, bufs, evac_at, out_waits):
        ybf, accA, accB = bufs

        def s0():
            sc.add("vector", lambda e: e.tensor_single_scalar(
                accA[:], ybf[:], THRESH[0], ALU.is_gt), waits=[evac_at])

        def mk(dst, src, k):
            def s():
                sc.add("vector", lambda e: e.scalar_tensor_tensor(
                    dst[:], ybf[:], THRESH[k], src[:], ALU.is_gt, ALU.add))
            return s

        def sf():
            sc.add("vector", lambda e: e.scalar_tensor_tensor(
                out_ap, accA[:], 0.1, x_res_ap, ALU.mult, ALU.add),
                waits=out_waits, incs=[(S_DVE, 1)])
            fb_guard[0] = S_DVE.n
        return [s0, mk(accB, accA, 1), mk(accA, accB, 2), mk(accB, accA, 3),
                mk(accA, accB, 4), sf]

    def phase_end(sc, exclude=()):
        for s in DSEMS:
            if s in exclude:
                continue
            sc.add(s.eng, lambda e: e.nop(), waits=[s.at()])

    def alloc_ffn_bufs(ph, wfc1_sb, wpj_sb, px):
        fb = {"wfc1": wfc1_sb, "wpj": wpj_sb}
        fb["xst"] = [ph.enter_context(nc.sbuf_tensor(f"{px}xst{i}", [128, 2 * DM], F32))
                     for i in range(3)]
        x1o = ph.enter_context(nc.sbuf_tensor(f"{px}x1o", [128, 2 * DM], F32))
        fb["x1o"] = [x1o, x1o]
        fb["xnbf"] = [ph.enter_context(nc.sbuf_tensor(f"{px}xnbf{i}", [128, DM], BF16))
                      for i in range(2)]
        fb["xnfm"] = [ph.enter_context(nc.sbuf_tensor(f"{px}xnfm{i}", [128, DC * 256], BF16))
                      for i in range(2)]
        fb["hrelu"] = [ph.enter_context(nc.sbuf_tensor(f"{px}hrelu{i}", [128, 256], BF16))
                       for i in range(2)]
        fb["hsq"] = [ph.enter_context(nc.sbuf_tensor(f"{px}hsq{i}", [128, 256], BF16))
                     for i in range(2)]
        fb["stats"] = {
            "st6": ph.enter_context(nc.sbuf_tensor(px + "sst6", [128, 48], F32)),
            "st2": ph.enter_context(nc.sbuf_tensor(px + "sst2", [128, 16], F32)),
            "rstd": ph.enter_context(nc.sbuf_tensor(px + "srstd", [128, 8], F32)),
            "lnt": ph.enter_context(nc.sbuf_tensor(px + "slnt", [128, 8], F32)),
        }
        fb["stair"] = tuple(ph.enter_context(
            nc.sbuf_tensor(f"{px}stb{k}", [128, 2 * DM], BF16)) for k in range(3))
        fb["pst"] = [ph.enter_context(nc.psum_tensor(f"{px}pst{i}", [128, DM], BF16))
                     for i in range(2)]
        fb["ps_h"] = [ph.enter_context(nc.psum_tensor(f"{px}ps_h{i}", [128, 512], F32))
                      for i in range(2)]
        fb["ps_y"] = [ph.enter_context(nc.psum_tensor(f"{px}ps_y{i}", [128, DM], F32))
                      for i in range(2)]
        fb["pst_free"] = [0, 0]
        fb["xnbf_free"] = [0, 0]
        fb["h_act"] = [0, 0]
        fb["hsq_done"] = [0, 0]
        fb["y_free"] = (S_ACT, 0)
        fb["slot_free"] = [[], [], []]
        fb["x1o_free"] = 0
        return fb

    def emit_ffn_phase(sc, fb, x_src_dram, y_dst_dram, n_tiles, w_wfc, w_wpj,
                       pre_fns=None, prelude_fn=None):
        wfcs3 = wfc_sh[:].rearrange("p (c f) -> p c f", c=DC)
        wfc13 = fb["wfc1"][:].rearrange("p (c f) -> p c f", c=DC)
        wpj3 = fb["wpj"][:].rearrange("p (c f) -> p c f", c=FC)
        xnfm2 = [fb["xnfm"][i][:].rearrange("p (c t) -> p c t", c=DC)
                 for i in range(2)]
        n_st = n_tiles // 2
        xn_ready = [[], [], []]
        st_ready = [None, None]
        ln_done = [None, None]

        def u_load(st, j):
            sl = st % 3
            if j == 0:
                xn_ready[sl] = []
            x_tile = fb["xst"][sl][:, j * DM:(j + 1) * DM]
            if pre_fns is None:
                tile = st * 2 + j
                sc.add("sync", lambda e, tile=tile, x_tile=x_tile: e.dma_start(
                    out=x_tile, in_=x_src_dram[tile * 128:(tile + 1) * 128, :]),
                    waits=list(fb["slot_free"][sl]), incs=[(XLD[sl], 16)])
            else:
                pre_fns[0](sc, st, j, fb["slot_free"][sl])

        def u_preln(st, j):
            sl = st % 3
            x_tile = fb["xst"][sl][:, j * DM:(j + 1) * DM]
            if pre_fns is None:
                st_ready[j] = [XLD[sl].at()]
            else:
                st_ready[j] = pre_fns[1](sc, st, j, x_tile)

        def u_ffnln(st, j):
            sl = st % 3
            x_tile = fb["xst"][sl][:, j * DM:(j + 1) * DM]
            emit_ln(sc, x_tile, fb["xnbf"][j][:], fb["stats"], (st * 2 + j) % 8,
                    st_ready[j] + [(S_PE, fb["xnbf_free"][j])])
            ln_done[j] = S_DVE.at()

        def u_trans(st, j):
            xnfm = xnfm2[st % 2]
            emit_transpose_tile(
                sc, lambda c, j=j: fb["xnbf"][j][:, c * 128:(c + 1) * 128],
                xnfm[:, :, j * 128:(j + 1) * 128], fb["pst"][j][:],
                [ln_done[j], (S_DVE, fb["pst_free"][j])])
            fb["pst_free"][j] = S_DVE.n
            fb["xnbf_free"][j] = S_PE.n
            xn_ready[st % 3].append(S_DVE.at())

        def stage1_micro(st):
            return [lambda: u_load(st, 0), lambda: u_load(st, 1),
                    lambda: u_preln(st, 0), lambda: u_preln(st, 1),
                    lambda: u_ffnln(st, 0), lambda: u_ffnln(st, 1),
                    lambda: u_trans(st, 0), lambda: u_trans(st, 1)]

        PLAN_F = [1, 3, 5, 11, 17, 22, 27, 29]

        def emit_mm1(f, xnfm, xr):
            hp = f % 2
            ps_h = fb["ps_h"][hp]
            hrelu = fb["hrelu"][hp]
            hsq = fb["hsq"][hp]

            w3 = wfcs3 if f < 16 else wfc13
            fo = f if f < 16 else f - 16

            def mm1(e, f=f, ps_h=ps_h, xnfm=xnfm, w3=w3, fo=fo):
                ins = None
                for d in range(DC):
                    ins = e.matmul(ps_h[:, 0:256],
                                   w3[:, d, fo * 128:(fo + 1) * 128],
                                   xnfm[:, d, :], start=(d == 0),
                                   stop=(d == DC - 1))
                return ins
            sc.add("tensor", mm1,
                   waits=list(xr) + [(WFC[f // 16], w_wfc[f // 16]),
                                     (S_ACT, fb["h_act"][hp])],
                   incs=[(S_PE, 1)])
            sc.add("scalar", lambda e, hrelu=hrelu, ps_h=ps_h: e.activation(
                hrelu[:], ps_h[:, 0:256], AF.Relu),
                waits=[S_PE.at()], incs=[(S_ACT, 1)])
            fb["h_act"][hp] = S_ACT.n
            sc.add("vector", lambda e, hsq=hsq, hrelu=hrelu: e.tensor_mul(
                hsq[:], hrelu[:], hrelu[:]),
                waits=[S_ACT.at()], incs=[(S_DVE, 1)])
            fb["hsq_done"][hp] = S_DVE.n

        def emit_mm2(f):
            hp = f % 2
            hsq = fb["hsq"][hp]
            w_mm2 = [(S_DVE, fb["hsq_done"][hp]),
                     (WPJ[f // 16], w_wpj[f // 16])]
            if f == 0:
                w_mm2.append(fb["y_free"])

            def mm2(e, f=f, hsq=hsq):
                ins = None
                for ts in range(2):
                    for dh in range(2):
                        ins = e.matmul(
                            fb["ps_y"][ts][:, dh * 512:(dh + 1) * 512],
                            hsq[:, ts * 128:(ts + 1) * 128],
                            wpj3[:, f, dh * 512:(dh + 1) * 512],
                            start=(f == 0), stop=(f == FC - 1))
                return ins
            sc.add("tensor", mm2, waits=w_mm2, incs=[(S_PE, 1)])

        def stage2(st, micro, plan_f):
            sl = st % 3
            hp2 = st % 2
            xst = fb["xst"][sl]
            x1o = fb["x1o"][hp2]
            xr = xn_ready[sl]
            mi = 0
            for f in range(FC):
                emit_mm1(f, xnfm2[st % 2], xr)
                if f >= 1:
                    emit_mm2(f - 1)
                while mi < len(micro) and f >= plan_f[mi]:
                    micro[mi]()
                    mi += 1
            while mi < len(micro):
                micro[mi]()
                mi += 1
            emit_mm2(FC - 1)
            mm2_done = S_PE.at()
            evac_at = emit_stair_evac(sc, [fb["ps_y"][0][:], fb["ps_y"][1][:]],
                                      fb["stair"], [mm2_done])
            fb["y_free"] = S_ACT.at()
            dve_steps = stair_dve_steps(
                sc, xst[:, 0:2 * DM], x1o[:, 0:2 * DM], fb["stair"], evac_at,
                [(X1O, fb["x1o_free"])])

            def ystore():
                sc.add("scalar", lambda e, st=st, x1o=x1o: e.dma_start(
                    out=y_dst_dram[st * 256:(st + 1) * 256, :].rearrange(
                        "(j p) d -> p j d", p=128),
                    in_=x1o[:, 0:2 * DM].rearrange("p (j d) -> p j d", j=2)),
                    waits=[S_DVE.at()], incs=[(X1O, 16)], dma=True)
                fb["slot_free"][sl] = [S_DVE.at(), S_ACT.at()]
                fb["x1o_free"] = X1O.n
            return dve_steps + [ystore]

        TAIL_F = [0, 0, 1, 2, 3, 4, 5]
        for k, s1 in enumerate(stage1_micro(0)):
            s1()
            if k == 1 and prelude_fn is not None:
                prelude_fn()
        tail = []
        for it in range(1, n_st + 1):
            micro = list(tail)
            plan = TAIL_F[:len(tail)]
            if it < n_st:
                micro += stage1_micro(it)
                plan = plan + [f + 6 for f in PLAN_F]
            tail = stage2(it - 1, micro, plan)
        for s in tail:
            s()

    with nc.Block() as blk:
        sc = Sched({'vector': S_DVE, 'scalar': S_ACT, 'gpsimd': S_GP})
        with ExitStack() as ph:
            wpj_sb = ph.enter_context(nc.sbuf_tensor("wpjA", [128, FC * DM], BF16))
            wfc1_sb = ph.enter_context(nc.sbuf_tensor("wfc1A",
                                                      [128, DC * (FF // 2)], BF16))
            load_consts(sc)
            load_wfc_sh(sc)
            load_wfc1(sc, wfc1_sb)
            load_wpj(sc, wpj_sb)
            fb = alloc_ffn_bufs(ph, wfc1_sb, wpj_sb, "a")
            emit_ffn_phase(sc, fb, x_in, x1_dram, NT,
                           (WFC[0].n, WFC[1].n), (WPJ[0].n, WPJ[1].n))
            phase_end(sc)
            sc.emit(blk)

    def debug_dump(srcs):
        with nc.Block() as blk:
            sc = Sched({'vector': S_DVE, 'scalar': S_ACT, 'gpsimd': S_GP})
            with ExitStack() as ph:
                buf = ph.enter_context(nc.sbuf_tensor("dbg", [128, DM], F32))
                n = 0
                for (src, rows, w) in srcs:
                    for t in range(rows // 128):
                        if n >= NT:
                            break
                        sc.add("gpsimd", lambda e, src=src, t=t, w=w: e.dma_start(
                            out=buf[:, 0:w], in_=src[t * 128:(t + 1) * 128, 0:w]),
                            waits=[CONG.at()], incs=[(CONG, 16)], dma=True)
                        sc.add("gpsimd", lambda e, n=n, w=w: e.dma_start(
                            out=y_out[n * 128:(n + 1) * 128, 0:w], in_=buf[:, 0:w]),
                            waits=[CONG.at()], incs=[(CONG, 16)], dma=True)
                        n += 1
                phase_end(sc)
                sc.emit(blk)
        es.close()
        return nc

    if upto == "A":
        return debug_dump([(x1_dram, LLOC, DM)])

    with nc.Block() as blk:
        sc = Sched({'vector': S_DVE, 'scalar': S_ACT, 'gpsimd': S_GP})
        with ExitStack() as ph:
            wqkv_sb = ph.enter_context(
                nc.sbuf_tensor("wqkvB", [128, DC * 3 * DM], BF16))
            wq3 = wqkv_sb[:].rearrange("p (c f) -> p c f", c=DC)
            wq_v = wqkv.rearrange("(c p) f -> p c f", p=128)
            for fam in range(3):
                fs = slice(fam * DM, (fam + 1) * DM)
                sc.add("gpsimd", lambda e, fs=fs: e.dma_start(
                    out=wq3[:, :, fs], in_=wq_v[:, :, fs]),
                    incs=[(WQKV[fam], 16)], dma=True)
            w_ld = WQKV[0].n

            bhat_all = ph.enter_context(nc.sbuf_tensor("bhat_all", [128, NT], F32))
            vscale_all = ph.enter_context(nc.sbuf_tensor("vscale_all", [128, NT], F32))
            ssm_all = ph.enter_context(nc.sbuf_tensor("ssm_all", [128, NT], F32))
            vs_tmp = ph.enter_context(nc.sbuf_tensor("vs_tmp", [128, NT], F32))
            id_f = ph.enter_context(nc.sbuf_tensor("id_f", [128, 128], F32))
            two_id = ph.enter_context(nc.sbuf_tensor("two_id", [128, 128], F32))
            masksu = ph.enter_context(nc.sbuf_tensor("masksu_s", [128, 128], F32))
            maskiu = ph.enter_context(nc.sbuf_tensor("maskiu_s", [128, 128], F32))
            for (dst, srcc) in ((id_f, ident_in), (two_id, twoid_in),
                                (masksu, masksu_in), (maskiu, maskiu_in)):
                sc.add("sync", lambda e, dst=dst, srcc=srcc: e.dma_start(
                    out=dst[:], in_=srcc), incs=[(CONS, 16)])
            xst = [ph.enter_context(nc.sbuf_tensor(f"bx{i}", [128, 2 * DM], F32))
                   for i in range(2)]
            xnbf = [ph.enter_context(nc.sbuf_tensor(f"bxn{i}", [128, DM], BF16))
                    for i in range(2)]
            xnfm_t = ph.enter_context(nc.sbuf_tensor("bxnfm", [128, DC * 256], BF16))
            xnfm = xnfm_t[:].rearrange("p (c t) -> p c t", c=DC)
            qfst_t = ph.enter_context(nc.sbuf_tensor("bqf", [128, DC * 256], BF16))
            qfst = qfst_t[:].rearrange("p (c t) -> p c t", c=DC)
            kfst_t = ph.enter_context(nc.sbuf_tensor("bkf", [128, DC * 256], BF16))
            kfst = kfst_t[:].rearrange("p (c t) -> p c t", c=DC)
            ktbf = [ph.enter_context(nc.sbuf_tensor(f"bkt{i}", [128, DM], BF16))
                    for i in range(2)]
            vtbf = [ph.enter_context(nc.sbuf_tensor(f"bvt{i}", [128, DM], BF16))
                    for i in range(2)]
            stats = {
                "st6": ph.enter_context(nc.sbuf_tensor("bst6", [128, 48], F32)),
                "st2": ph.enter_context(nc.sbuf_tensor("bst2", [128, 16], F32)),
                "rstd": ph.enter_context(nc.sbuf_tensor("brstd", [128, 8], F32)),
                "lnt": ph.enter_context(nc.sbuf_tensor("blnt", [128, 8], F32)),
                "junk": ph.enter_context(nc.sbuf_tensor("bst_junk", [128, DM], F32)),
            }
            kss = ph.enter_context(nc.sbuf_tensor("bkss", [128, 8], F32))
            betas = ph.enter_context(nc.sbuf_tensor("bbetas", [128, 8], F32))
            rss = ph.enter_context(nc.sbuf_tensor("brss", [128, 8], F32))
            e_f = ph.enter_context(nc.sbuf_tensor("be_f", [128, 128], F32))
            mt_b = ph.enter_context(nc.sbuf_tensor("bmt", [128, 128], BF16))
            tm_b = ph.enter_context(nc.sbuf_tensor("btm", [128, 128], BF16))
            x_nb = [ph.enter_context(nc.sbuf_tensor(f"bxn2_{i}", [128, 128], BF16))
                    for i in range(2)]
            tx_b = ph.enter_context(nc.sbuf_tensor("btx", [128, 128], BF16))
            d_b = ph.enter_context(nc.sbuf_tensor("bd", [128, 128], BF16))
            xl_b = ph.enter_context(nc.sbuf_tensor("bxl", [128, 256], BF16))
            pst = ph.enter_context(nc.psum_tensor("bpst", [128, DM], BF16))
            ps_n = ph.enter_context(nc.psum_tensor("bpsn", [128, 512], F32))
            ps_q = [ph.enter_context(nc.psum_tensor(f"bpsq{i}", [128, 512], F32))
                    for i in range(2)]
            ps_kv = [ph.enter_context(nc.psum_tensor(f"bpskv{i}", [128, DM], F32))
                     for i in range(2)]

            pst_free = 0
            xnbf_free = [0, 0]
            q_evac = [0, 0]
            kv_evac = [0, 0]
            kv_dve = [0, 0]
            beta_done = 0
            slot_free = [[], []]
            for st in range(NT // 2):
                sl = st % 2
                xn_ready = []
                for j in range(2):
                    tile = st * 2 + j
                    x_tile = xst[sl][:, j * DM:(j + 1) * DM]
                    sc.add("sync", lambda e, tile=tile, x_tile=x_tile: e.dma_start(
                        out=x_tile, in_=x1_dram[tile * 128:(tile + 1) * 128, :]),
                        waits=list(slot_free[sl]), incs=[(XLD[sl], 16)])
                for j in range(2):
                    tile = st * 2 + j
                    x_tile = xst[sl][:, j * DM:(j + 1) * DM]
                    emit_ln(sc, x_tile, xnbf[j][:], stats, tile % 8,
                            [XLD[sl].at(), (S_PE, xnbf_free[j])])
                    ln_done = S_DVE.at()
                    emit_transpose_tile(
                        sc, lambda c, j=j: xnbf[j][:, c * 128:(c + 1) * 128],
                        xnfm[:, :, j * 128:(j + 1) * 128], pst[:],
                        [ln_done, (S_DVE, pst_free)])
                    pst_free = S_DVE.n
                    xnbf_free[j] = S_PE.n
                    xn_ready.append(S_DVE.at())
                slot_free[sl] = [S_DVE.at(), S_ACT.at()]
                for dco in range(DC):
                    qp = dco % 2

                    def qmm(e, dco=dco, qp=qp):
                        ins = None
                        for d in range(DC):
                            ins = e.matmul(ps_q[qp][:, 0:256],
                                           wq3[:, d, dco * 128:(dco + 1) * 128],
                                           xnfm[:, d, :], start=(d == 0),
                                           stop=(d == DC - 1))
                        return ins
                    sc.add("tensor", qmm,
                           waits=list(xn_ready) + [(WQKV[0], w_ld),
                                                   (S_DVE, q_evac[qp]),
                                                   (S_ACT, beta_done)],
                           incs=[(S_PE, 1)])
                    sc.add("vector", lambda e, dco=dco, qp=qp: e.tensor_scalar_mul(
                        qfst[:, dco, :], ps_q[qp][:, 0:256], 1.0 / 32.0),
                        waits=[S_PE.at(), QST.at()], incs=[(S_DVE, 1)])
                    q_evac[qp] = S_DVE.n
                sc.add("scalar", lambda e, st=st: e.dma_start(
                    out=qfw[st // 4][:, :, (st % 4) * 256:(st % 4 + 1) * 256],
                    in_=qfst_t[:].rearrange("p (c t) -> p c t", c=DC)),
                    waits=[S_DVE.at()], incs=[(QST, 16)], dma=True)
                for j in range(2):
                    tile = st * 2 + j

                    def kmm(e, j=j):
                        ins = None
                        for dh in range(2):
                            for d in range(DC):
                                ins = e.matmul(
                                    ps_kv[j][:, dh * 512:(dh + 1) * 512],
                                    xnfm[:, d, j * 128:(j + 1) * 128],
                                    wq3[:, d, DM + dh * 512:DM + (dh + 1) * 512],
                                    start=(d == 0), stop=(d == DC - 1))
                        return ins
                    sc.add("tensor", kmm,
                           waits=list(xn_ready) + [(WQKV[1], 16),
                                                   (S_ACT, kv_evac[j]),
                                                   (S_DVE, kv_dve[j])],
                           incs=[(S_PE, 1)])
                    kmm_done = S_PE.at()
                    sc.add("vector", lambda e, j=j: e.tensor_copy(
                        ktbf[j][:], ps_kv[j][:]),
                        waits=[kmm_done, KTST[j].at()], incs=[(S_DVE, 1)])
                    kt_done = S_DVE.at()
                    kv_dve[j] = S_DVE.n
                    sc.add("scalar", lambda e, j=j, tile=tile: e.activation(
                        stats["junk"][:], ps_kv[j][:], AF.Square,
                        accum_out=kss[:, tile % 8:tile % 8 + 1]),
                        waits=[kmm_done, kt_done], incs=[(S_ACT, 1)])
                    kv_evac[j] = S_ACT.n
                    g = tile // 8
                    sc.add("scalar", lambda e, j=j, g=g, tile=tile: e.dma_start(
                        out=ktw[g][:, tile % 8, :], in_=ktbf[j][:]),
                        waits=[kt_done], incs=[(KTST[j], 16)], dma=True)
                    emit_transpose_tile(
                        sc, lambda c, j=j: ktbf[j][:, c * 128:(c + 1) * 128],
                        kfst[:, :, j * 128:(j + 1) * 128], pst[:],
                        [kt_done, (S_DVE, pst_free), KFST.at()])
                    pst_free = S_DVE.n
                sc.add("scalar", lambda e, st=st: e.dma_start(
                    out=kfw[st // 4][:, :, (st % 4) * 256:(st % 4 + 1) * 256],
                    in_=kfst_t[:].rearrange("p (c t) -> p c t", c=DC)),
                    waits=[S_DVE.at()], incs=[(KFST, 16)], dma=True)
                for j in range(2):
                    tile = st * 2 + j

                    def vmm(e, j=j):
                        ins = None
                        for dh in range(2):
                            for d in range(DC):
                                ins = e.matmul(
                                    ps_kv[j][:, dh * 512:(dh + 1) * 512],
                                    xnfm[:, d, j * 128:(j + 1) * 128],
                                    wq3[:, d, 2 * DM + dh * 512:2 * DM + (dh + 1) * 512],
                                    start=(d == 0), stop=(d == DC - 1))
                        return ins
                    sc.add("tensor", vmm,
                           waits=[(WQKV[2], 16), (S_ACT, kv_evac[j]),
                                  (S_DVE, kv_dve[j])],
                           incs=[(S_PE, 1)])
                    sc.add("scalar", lambda e, j=j: e.activation(
                        vtbf[j][:], ps_kv[j][:], AF.Silu),
                        waits=[S_PE.at(), VTST[j].at()], incs=[(S_ACT, 1)])
                    kv_evac[j] = S_ACT.n
                    for dh in range(2):
                        sc.add("scalar", lambda e, j=j, g=tile // 8, jj=tile % 8,
                               dh=dh: e.dma_start(
                            out=vw[g][:, dh, jj, :],
                            in_=vtbf[j][:, dh * 512:(dh + 1) * 512]),
                            waits=[S_ACT.at()], incs=[(VTST[j], 16)], dma=True)
                for j in range(2):
                    tile = st * 2 + j
                    s8 = slice(tile % 8, tile % 8 + 1)
                    ci = slice(tile, tile + 1)

                    def bmm(e, j=j):
                        ins = None
                        for d in range(DC):
                            ins = e.matmul(ps_q[0][:, 256 + j:256 + j + 1],
                                           xnfm[:, d, j * 128:(j + 1) * 128],
                                           wb_sb[:, d:d + 1],
                                           start=(d == 0), stop=(d == DC - 1))
                        return ins
                    sc.add("tensor", bmm,
                           waits=[(S_DVE, q_evac[0]), (CONG, 80),
                                  (S_ACT, beta_done)],
                           incs=[(S_PE, 1)])
                    sc.add("scalar", lambda e, j=j, s8=s8: e.activation(
                        betas[:, s8], ps_q[0][:, 256 + j:256 + j + 1], AF.Sigmoid),
                        waits=[S_PE.at()], incs=[(S_ACT, 1)])
                    beta_done = S_ACT.n
                    sc.add("vector", lambda e, s8=s8, ci=ci: e.tensor_single_scalar(
                        ssm_all[:, ci], kss[:, s8], 1e-24, ALU.max),
                        waits=[(S_ACT, kv_evac[j])], incs=[(S_DVE, 1)])
                    sc.add("vector", lambda e, s8=s8, ci=ci: e.reciprocal(
                        rss[:, s8], ssm_all[:, ci]),
                        waits=[S_DVE.at()], incs=[(S_DVE, 1)])
                    sc.add("vector", lambda e, s8=s8, ci=ci: e.tensor_mul(
                        bhat_all[:, ci], betas[:, s8], rss[:, s8]),
                        waits=[S_DVE.at(), (S_ACT, beta_done)], incs=[(S_DVE, 1)])
                for j in range(2):
                    tile = st * 2 + j
                    bh = bhat_all[:, tile:tile + 1]

                    def gmm(e, j=j):
                        ins = None
                        for d in range(DC):
                            kfs = kfst[:, d, j * 128:(j + 1) * 128]
                            ins = e.matmul(ps_n[:, 0:128], kfs, kfs,
                                           start=(d == 0), stop=(d == DC - 1))
                        return ins
                    sc.add("tensor", gmm, waits=[(S_DVE, S_DVE.n), (CONS, 64)],
                           incs=[(S_PE, 1)])
                    sc.add("vector", lambda e, bh=bh: e.scalar_tensor_tensor(
                        e_f[:], ps_n[:, 0:128], bh, masksu[:], ALU.mult, ALU.mult),
                        waits=[S_PE.at()], incs=[(S_DVE, 1)])
                    sc.add("vector", lambda e: e.tensor_sub(x_nb[0][:], id_f[:],
                                                            e_f[:]))
                    sc.add("vector", lambda e: e.tensor_add(mt_b[:], id_f[:],
                                                            e_f[:]),
                           incs=[(S_DVE, 1)])
                    sc.add("tensor", lambda e: e.transpose(pst[:, 0:128], mt_b[:],
                                                           id_b[:]),
                           waits=[(S_DVE, S_DVE.n), (S_DVE, pst_free)],
                           incs=[(S_PE, 1)])
                    sc.add("vector", lambda e: e.tensor_copy(tm_b[:], pst[:, 0:128]),
                           waits=[S_PE.at()], incs=[(S_DVE, 1)])
                    pst_free = S_DVE.n

                    def lmm(e, j=j):
                        ins = None
                        for d in range(DC):
                            ins = e.matmul(ps_n[:, 128:256],
                                           kfst[:, d, j * 128:(j + 1) * 128],
                                           qfst[:, d, j * 128:(j + 1) * 128],
                                           start=(d == 0), stop=(d == DC - 1))
                        return ins
                    sc.add("tensor", lmm, waits=[(S_DVE, S_DVE.n)], incs=[(S_PE, 1)])
                    sc.add("vector", lambda e: e.tensor_mul(xl_b[:, 128:256],
                                                            ps_n[:, 128:256],
                                                            maskiu[:]),
                           waits=[S_PE.at(), XLST[0].at(), XLST[1].at()],
                           incs=[(S_DVE, 1)])
                    xc = 0
                    for it in range(6):
                        sc.add("tensor", lambda e, xc=xc: e.matmul(
                            ps_n[:, 256:384], tm_b[:], x_nb[xc][:],
                            start=True, stop=True),
                            waits=[(S_DVE, S_DVE.n)], incs=[(S_PE, 1)])
                        sc.add("vector", lambda e: e.tensor_sub(
                            d_b[:], two_id[:], ps_n[:, 256:384]),
                            waits=[S_PE.at()], incs=[(S_DVE, 1)])
                        sc.add("tensor", lambda e, xc=xc: e.transpose(
                            pst[:, 128:256], x_nb[xc][:], id_b[:]),
                            incs=[(S_PE, 1)])
                        sc.add("vector", lambda e: e.tensor_copy(
                            tx_b[:], pst[:, 128:256]),
                            waits=[S_PE.at()], incs=[(S_DVE, 1)])
                        sc.add("tensor", lambda e: e.matmul(
                            ps_n[:, 384:512], tx_b[:], d_b[:],
                            start=True, stop=True),
                            waits=[(S_DVE, S_DVE.n)], incs=[(S_PE, 1)])
                        dst = xl_b[:, 0:128] if it == 5 else x_nb[1 - xc][:]
                        sc.add("vector", lambda e, dst=dst: e.tensor_copy(
                            dst, ps_n[:, 384:512]),
                            waits=[S_PE.at()], incs=[(S_DVE, 1)])
                        xc = 1 - xc
                    sc.add("scalar", lambda e, g=tile // 8, jj=tile % 8:
                           e.dma_start(
                        out=xlw[g][:, jj // 4, (jj % 4) * 256:(jj % 4) * 256 + 256],
                        in_=xl_b[:]),
                        waits=[(S_DVE, S_DVE.n)], incs=[(XLST[j], 16)], dma=True)
                if st == 3 or st == 7:
                    g = st // 4
                    hs = slice(g * 8, g * 8 + 8)
                    sc.add("scalar", lambda e, hs=hs: e.activation(
                        vs_tmp[:, hs], ssm_all[:, hs], AF.Ln),
                        waits=[(S_DVE, S_DVE.n)], incs=[(S_ACT, 1)])
                    sc.add("scalar", lambda e, hs=hs: e.activation(
                        vscale_all[:, hs], vs_tmp[:, hs], AF.Exp, scale=0.5),
                        waits=[S_ACT.at()], incs=[(S_ACT, 1)])
                    sc.add("scalar", lambda e, g=g, hs=hs: e.dma_start(
                        out=cc1c_in.ap()[g * 128:(g + 1) * 128, 0:8],
                        in_=bhat_all[:, hs]),
                        waits=[(S_DVE, S_DVE.n)], incs=[(QST, 16)], dma=True)
                    sc.add("scalar", lambda e, g=g, hs=hs: e.dma_start(
                        out=cc1c_in.ap()[g * 128:(g + 1) * 128, 8:16],
                        in_=vscale_all[:, hs]),
                        waits=[S_ACT.at()], incs=[(QST, 16)], dma=True)
                    sc.add("scalar", lambda e: e.nop(),
                           waits=[QST.at(), KFST.at(), KTST[0].at(), KTST[1].at(),
                                  VTST[0].at(), VTST[1].at(), XLST[0].at(),
                                  XLST[1].at()],
                           incs=[(BRDY, 1)], dma=True)
                    sc.add("gpsimd", lambda e, g=g: e.collective_compute(
                        "AllGather", ALU.bypass, replica_groups=[list(range(8))],
                        ins=[cc1_in.ap()[g * HBLK:(g + 1) * HBLK, :].opt()],
                        outs=[cc1_out.ap()[g * 8 * HBLK:(g + 1) * 8 * HBLK, :].opt()]),
                        waits=[(BRDY, g + 1)], incs=[(CCA, 1)], dma=True)
                    sc.add("gpsimd", lambda e, g=g: e.collective_compute(
                        "AllGather", ALU.bypass, replica_groups=[list(range(8))],
                        ins=[ccv_in.ap()[g * 2048:(g + 1) * 2048, :].opt()],
                        outs=[ccv_out.ap()[g * 8 * 2048:(g + 1) * 8 * 2048, :].opt()]),
                        incs=[(CCB, 1)], dma=True)
                    sc.add("gpsimd", lambda e, g=g: e.collective_compute(
                        "AllGather", ALU.bypass, replica_groups=[list(range(8))],
                        ins=[cc1c_in.ap()[g * 128:(g + 1) * 128, :].opt()],
                        outs=[cc1c_out.ap()[g * 1024:(g + 1) * 1024, :].opt()]),
                        incs=[(CC, 1)], dma=True)
            phase_end(sc, exclude=(CC, CCA, CCB))
            sc.emit(blk)

    if upto == "B":
        return debug_dump([(cc1v, LLOC, 1024)])

    with nc.Block() as blk:
        sc = Sched({'vector': S_DVE, 'scalar': S_ACT, 'gpsimd': S_GP})
        with ExitStack() as ph:
            sbq = [ph.enter_context(nc.sbuf_tensor(f"csbq{i}", [128, DC * 1024], BF16))
                   for i in range(2)]
            sbk = [ph.enter_context(nc.sbuf_tensor(f"csbk{i}", [128, DC * 1024], BF16))
                   for i in range(2)]
            sbkt = [ph.enter_context(nc.sbuf_tensor(f"cskt{i}", [128, DC * 1024],
                                                     BF16)) for i in range(2)]
            sbxl = [ph.enter_context(nc.sbuf_tensor(f"csxl{i}", [128, 2048], BF16))
                    for i in range(2)]
            sbv = [ph.enter_context(nc.sbuf_tensor(f"csv{i}", [128, DC * DV], BF16))
                   for i in range(2)]
            r_b = ph.enter_context(nc.sbuf_tensor("cr", [128, DV], BF16))
            u_b = ph.enter_context(nc.sbuf_tensor("cu", [128, DV], BF16))
            o_sb = [ph.enter_context(nc.sbuf_tensor(f"co{i}", [128, DV], BF16))
                    for i in range(2)]
            S_f = ph.enter_context(nc.sbuf_tensor("cSf", [128, DC * DV], F32))
            S_bf = ph.enter_context(nc.sbuf_tensor("cSb", [128, DC * DV], BF16))
            Sf3 = S_f[:].rearrange("p (c f) -> p c f", c=DC)
            Sb3 = S_bf[:].rearrange("p (c f) -> p c f", c=DC)
            ps_r = ph.enter_context(nc.psum_tensor("cpsr", [128, DV], F32))
            ps_k2 = ph.enter_context(nc.psum_tensor("cpsk2", [128, DV], F32))
            ps_o = ph.enter_context(nc.psum_tensor("cpso", [128, DV], F32))

            def emit_bh2(blkhalf):
                sc.add("gpsimd", lambda e, blkhalf=blkhalf: e.indirect_dma_start(
                    bh2[:, blkhalf * 16:(blkhalf + 1) * 16], None, cc1c_out.ap(),
                    IndirectOffsetOnAxis(idxbh[:, blkhalf:blkhalf + 1], 0)),
                    waits=[(CC, 1 if blkhalf % 2 == 0 else 2), (CONG, 80)],
                    incs=[(BH2S[blkhalf], 16)], dma=True)

            emit_bh2(0)
            emit_bh2(2)

            def emit_stage_gathers(s, guard_val=None):
                sb = s % 2
                ccw = 1 if s < 2 else 2
                guard = [(S_PE, guard_val)] if guard_val is not None else []
                fams = [(sbq[sb], cc1_out, DC * 1024), (sbk[sb], cc1_out, DC * 1024),
                        (sbkt[sb], cc1_out, DC * 1024), (sbxl[sb], cc1_out, 2048),
                        (sbv[sb], ccv_out, DC * DV)]
                for fam, (dst, srct, w) in enumerate(fams):
                    icol = s * 5 + fam
                    sc.add("gpsimd", lambda e, dst=dst, srct=srct, w=w, icol=icol:
                           e.indirect_dma_start(
                               dst[:, 0:w], None, srct.ap(),
                               IndirectOffsetOnAxis(idxst[:, icol:icol + 1], 0)),
                           waits=[(CC, ccw), (CONG, 80)] + guard,
                           incs=[(SQG[sb], 16)], dma=True)

            emit_stage_gathers(0)
            emit_stage_gathers(1)
            s_cast_done = 0
            r_evac = 0
            o_evac = 0
            stage_guard = [None, None]
            for i in range(NCHUNK):
                if i == 12 or i == 16:
                    stage_guard[0 if i == 12 else 1] = S_PE.n
                cb = i % 2
                sb = (i // 4) % 2
                j = ((i // 8) % 2) * 4 + i % 4
                qf3 = sbq[sb][:].rearrange("p (c t) -> p c t", c=DC)
                kf3 = sbk[sb][:].rearrange("p (c t) -> p c t", c=DC)
                sqw = (SQG[sb], 80 * (i // 16 + 1))
                xoff = (j // 4) * 1024 + (j % 4) * 256
                if i == 13:
                    emit_stage_gathers(2, stage_guard[0])
                if i == 15:
                    emit_bh2(1)
                    emit_bh2(3)
                if i == 17:
                    emit_stage_gathers(3, stage_guard[1])
                ld_w = [sqw]
                bhcol = ((i // 4) % 2) * 2 + (i // 16)
                bh = bh2[:, bhcol * 16 + j:bhcol * 16 + j + 1]
                vs = bh2[:, bhcol * 16 + 8 + j:bhcol * 16 + 8 + j + 1]

                bhw = (BH2S[bhcol], 16)
                if i > 0:
                    def ks0(e, kf3=kf3, j=j):
                        ins = None
                        for d in range(DC):
                            ins = e.matmul(ps_r[:], kf3[:, d, j * 128:(j + 1) * 128],
                                           Sb3[:, d, :],
                                           start=(d == 0), stop=(d == DC - 1))
                        return ins
                    sc.add("tensor", ks0, waits=ld_w + [(S_ACT, s_cast_done),
                                                        (S_DVE, r_evac)],
                           incs=[(S_PE, 1)])
                    sc.add("vector", lambda e, sb=sb, j=j, vs=vs:
                           e.scalar_tensor_tensor(
                        r_b[:], sbv[sb][:, j * DV:(j + 1) * DV], vs, ps_r[:],
                        ALU.mult, ALU.subtract),
                        waits=[S_PE.at(), (CONG, 80), bhw], incs=[(S_DVE, 1)])
                else:
                    sc.add("vector", lambda e, sb=sb, j=j, vs=vs: e.tensor_scalar(
                        r_b[:], sbv[sb][:, j * DV:(j + 1) * DV], vs, None,
                        ALU.mult),
                        waits=ld_w + [(CONG, 80), bhw], incs=[(S_DVE, 1)])
                r_evac = S_DVE.n
                if i > 0:
                    def qs0(e, qf3=qf3, j=j):
                        ins = None
                        for d in range(DC):
                            ins = e.matmul(ps_o[:], qf3[:, d, j * 128:(j + 1) * 128],
                                           Sb3[:, d, :],
                                           start=(d == 0), stop=False)
                        return ins
                    sc.add("tensor", qs0, waits=ld_w + [(S_ACT, s_cast_done),
                                                        (S_DVE, o_evac)],
                           incs=[(S_PE, 1)])
                sc.add("tensor", lambda e, sb=sb, xoff=xoff: e.matmul(
                    ps_r[:], sbxl[sb][:, xoff:xoff + 128], r_b[:],
                    start=True, stop=True),
                    waits=[(S_DVE, r_evac)], incs=[(S_PE, 1)])
                sc.add("vector", lambda e, bh=bh: e.tensor_scalar(
                    u_b[:], ps_r[:], bh, None, ALU.mult),
                    waits=[S_PE.at()], incs=[(S_DVE, 1)])
                u_done = S_DVE.at()
                def lumm(e, sb=sb, xoff=xoff, first=(i == 0)):
                    return e.matmul(ps_o[:], sbxl[sb][:, xoff + 128:xoff + 256],
                                    u_b[:], start=first, stop=True)
                lu_w = [u_done]
                if i == 0:
                    lu_w.append((S_DVE, o_evac))
                sc.add("tensor", lumm, waits=lu_w, incs=[(S_PE, 1)])
                sc.add("vector", lambda e, cb=cb: e.tensor_copy(o_sb[cb][:], ps_o[:]),
                       waits=[S_PE.at(), OST[cb].at()], incs=[(S_DVE, 1)])
                o_evac = S_DVE.n
                sc.add("scalar", lambda e, i=i, cb=cb: e.dma_start(
                    out=cc2_in.ap()[i * 128:(i + 1) * 128, :], in_=o_sb[cb][:]),
                    waits=[S_DVE.at()], incs=[(OST[cb], 16)], dma=True)
                def kumm_fn(d, ps):
                    def kumm(e, d=d, sb=sb, j=j, ps=ps):
                        return e.matmul(
                            ps[:], sbkt[sb][:, j * 1024 + d * 128:
                                            j * 1024 + (d + 1) * 128],
                            u_b[:], start=True, stop=True)
                    return kumm

                def s_add(d, ps, pe_at):
                    if i == 0:
                        sc.add("vector", lambda e, d=d, ps=ps: e.tensor_copy(
                            Sf3[:, d, :], ps[:]),
                            waits=[pe_at], incs=[(S_DVE, 1)])
                    else:
                        sc.add("vector", lambda e, d=d, ps=ps: e.tensor_add(
                            Sf3[:, d, :], Sf3[:, d, :], ps[:]),
                            waits=[pe_at], incs=[(S_DVE, 1)])
                    sc.add("scalar", lambda e, d=d: e.activation(
                        Sb3[:, d, :], Sf3[:, d, :], AF.Copy),
                        waits=[S_DVE.at()], incs=[(S_ACT, 1)])

                kps = [ps_r, ps_k2]
                pend = []
                for d in range(DC):
                    ps = kps[d % 2]
                    sc.add("tensor", kumm_fn(d, ps),
                           waits=[(S_DVE, S_DVE.n)], incs=[(S_PE, 1)])
                    pend.append((d, ps, S_PE.at()))
                    if len(pend) == 2:
                        dd, pp, pa = pend.pop(0)
                        s_add(dd, pp, pa)
                for (dd, pp, pa) in pend:
                    s_add(dd, pp, pa)
                r_evac = S_DVE.n
                s_cast_done = S_ACT.n

                if i % 8 == 7:
                    pp = i // 8
                    sc.add("scalar", lambda e: e.nop(),
                           waits=[OST[0].at(), OST[1].at()],
                           incs=[(BRDY, 1)], dma=True)
                    sc.add("gpsimd", lambda e, pp=pp: e.collective_compute(
                        "AllGather", ALU.bypass, replica_groups=[list(range(8))],
                        ins=[cc2_in.ap()[pp * 1024:(pp + 1) * 1024, :].opt()],
                        outs=[cc2_out.ap()[pp * 8 * 1024:(pp + 1) * 8 * 1024,
                                           :].opt()]),
                        waits=[(BRDY, 3 + pp)], incs=[(CC, 1)], dma=True)
            phase_end(sc, exclude=(CC, CCA, CCB))
            sc.emit(blk)

    if upto == "C":
        return debug_dump([(cc2_in.ap(), L, DV)])

    with nc.Block() as blk:
        sc = Sched({'vector': S_DVE, 'scalar': S_ACT, 'gpsimd': S_GP})
        with ExitStack() as ph:
            wpj_sb = ph.enter_context(nc.sbuf_tensor("wpjD", [128, FC * DM], BF16))
            wfc1_sb = ph.enter_context(nc.sbuf_tensor("wfc1D",
                                                      [128, DC * (FF // 2)], BF16))
            fb = alloc_ffn_bufs(ph, wfc1_sb, wpj_sb, "d")
            g_sb = ph.enter_context(nc.sbuf_tensor("g_sb", [128, DM], F32))
            b_sb = ph.enter_context(nc.sbuf_tensor("b_sb", [128, DM], F32))
            for (dst, srcc) in ((g_sb, gfull), (b_sb, bfull)):
                sc.add("sync", lambda e, dst=dst, srcc=srcc: e.dma_start(
                    out=dst[:], in_=srcc), incs=[(CONS, 16)])
            x1t = [ph.enter_context(nc.sbuf_tensor(f"dx1_{i}", [128, DM], F32))
                   for i in range(2)]
            obf = [ph.enter_context(nc.sbuf_tensor(f"dob_{i}", [128, DM], BF16))
                   for i in range(2)]
            d_used = [[], []]

            def pre_load(sc, st, j, slot_free):
                tile = st * 2 + j
                guards = list(d_used[j]) + list(slot_free)
                sc.add("sync", lambda e, tile=tile, j=j: e.dma_start(
                    out=x1t[j][:], in_=x1_dram[tile * 128:(tile + 1) * 128, :]),
                    waits=guards, incs=[(DXL[j], 16)])
                for half in range(2):
                    icol = half * 16 + tile
                    sc.add("gpsimd", lambda e, j=j, half=half, icol=icol:
                           e.indirect_dma_start(
                               obf[j][:, half * DV:(half + 1) * DV], None,
                               cc2_out.ap(),
                               IndirectOffsetOnAxis(idxo[:, icol:icol + 1], 0)),
                           waits=guards + [(CC, 3 + tile // 4),
                                           (CONG, CONG.n)],
                           incs=[(DLD[j], 16)], dma=True)

            def pre_ln(sc, st, j, x_tile):
                tile = st * 2 + j
                ld = [DXL[j].at(), DLD[j].at()]
                emit_ln(sc, obf[j][:], x_tile, fb["stats"], tile % 8, ld)
                xn_done = S_DVE.at()
                sc.add("gpsimd", lambda e, x_tile=x_tile: e.tensor_mul(
                    x_tile, x_tile, g_sb[:]),
                    waits=[xn_done, (CONS, 96)])
                sc.add("gpsimd", lambda e, x_tile=x_tile: e.tensor_add(
                    x_tile, x_tile, b_sb[:]))
                sc.add("gpsimd", lambda e, j=j, x_tile=x_tile: e.tensor_add(
                    x_tile, x_tile, x1t[j][:]),
                    waits=[DXL[j].at()], incs=[(S_GP, 1)])
                d_used[j] = [xn_done, S_GP.at(), S_ACT.at()]
                return [S_GP.at()]

            emit_ffn_phase(sc, fb, None, y_out, NT,
                           (WFC[0].n, WFC[1].n + 16),
                           (WPJ[0].n + 16, WPJ[1].n + 16),
                           pre_fns=(pre_load, pre_ln),
                           prelude_fn=lambda: (load_wpj(sc, wpj_sb),
                                               load_wfc1(sc, wfc1_sb)))
            phase_end(sc)
            sc.emit(blk)

    es.close()
    return nc


def prep_in_maps(L0_attn_x, L0_attn_Wq, L0_attn_Wk, L0_attn_Wv, L0_attn_Wb,
                 L0_attn_ln_g, L0_attn_ln_b, L0_ffn_W_fc, L0_ffn_W_proj):
    x = np.ascontiguousarray(np.asarray(L0_attn_x, dtype=np.float32))
    wfcT_h = np.ascontiguousarray(np.asarray(L0_ffn_W_fc, np.float32).T)
    wpjT_h = np.ascontiguousarray(np.asarray(L0_ffn_W_proj, np.float32).T)
    Wq = np.asarray(L0_attn_Wq, np.float32)
    Wk = np.asarray(L0_attn_Wk, np.float32)
    Wv = np.asarray(L0_attn_Wv, np.float32)
    wb_h = np.ascontiguousarray(
        np.asarray(L0_attn_Wb, np.float32).reshape(DC, 128).T)
    gfull_h = np.ascontiguousarray(np.broadcast_to(
        np.asarray(L0_attn_ln_g, np.float32), (128, DM)))
    bfull_h = np.ascontiguousarray(np.broadcast_to(
        np.asarray(L0_attn_ln_b, np.float32), (128, DM)))
    ident_h = np.eye(128, dtype=np.float32)
    twoid_h = 2.0 * np.eye(128, dtype=np.float32)
    masksu_h = np.triu(np.ones((128, 128), np.float32), 1)
    maskiu_h = np.triu(np.ones((128, 128), np.float32), 0)

    p = np.arange(128, dtype=np.int64)
    in_maps = []
    for c in range(8):
        b, h = c // 2, c % 2
        wqkv_h = np.ascontiguousarray(np.concatenate([Wq, Wk, Wv], axis=1))
        HB, G8 = 3328, 8 * 3328
        idxst = np.zeros((128, 20), np.int64)
        for s in range(4):
            rank, g = 2 * b + s % 2, s // 2
            base = g * G8 + rank * HB
            idxst[:, s * 5 + 0] = base + 1024 + p * 8
            idxst[:, s * 5 + 1] = base + 2048 + p * 8
            idxst[:, s * 5 + 2] = base + p * 8
            idxst[:, s * 5 + 3] = base + 3072 + p * 2
            idxst[:, s * 5 + 4] = (g * 8 * 2048 + rank * 2048
                                   + p * 16 + h * 8)
        idxbh = np.zeros((128, 4), np.int64)
        for blk in range(2):
            for g in range(2):
                idxbh[:, blk * 2 + g] = g * 1024 + (2 * b + blk) * 128 + p
        idxo = np.zeros((128, 32), np.int64)
        for q in range(2):
            for t in range(NT):
                idxo[:, q * 16 + t] = ((t // 4) * 8 * 1024 + (2 * b + q) * 1024
                                       + h * 512 + (t % 4) * 128 + p)
        in_maps.append({
            "x": np.ascontiguousarray(np.concatenate(
                [x[b, (2 * el + h) * 512:(2 * el + h + 1) * 512]
                 for el in range(4)], axis=0)),
            "wfcT": wfcT_h, "wpjT": wpjT_h, "wqkv": wqkv_h, "wb": wb_h,
            "gfull": gfull_h, "bfull": bfull_h, "ident": ident_h,
            "twoid": twoid_h, "masksu": masksu_h, "maskiu": maskiu_h,
            "idxst": idxst.astype(np.int32),
            "idxbh": idxbh.astype(np.int32), "idxo": idxo.astype(np.int32),
        })
    return in_maps


_NC_CACHE = {}


def kernel(**inputs):
    in_maps = prep_in_maps(**inputs)
    if "nc" not in _NC_CACHE:
        _NC_CACHE["nc"] = build_nc()
    nc = _NC_CACHE["nc"]
    res = run_bass_kernel_spmd(nc, in_maps, core_ids=list(range(8)))
    out = np.empty((BB, L, DM), np.float32)
    for c in range(8):
        b, h = c // 2, c % 2
        y = res.results[c]["y"]
        for el in range(4):
            e8 = 2 * el + h
            out[b, e8 * 512:(e8 + 1) * 512] = y[el * 512:(el + 1) * 512]
    return out
